# version 09
# baseline (speedup 1.0000x reference)
import numpy as np
from contextlib import ExitStack
import concourse.bass as bass
import concourse.mybir as mb
from concourse.bass_utils import run_bass_kernel_spmd

F32 = mb.dt.float32
BF16 = mb.dt.bfloat16
FP8 = mb.dt.float8e4
AF = mb.ActivationFunctionType
OP = mb.AluOpType

NCORES = 8
T = 1024
NTC = T // 128
D = 2048
NDC = D // 128
LN_EPS = 1e-5
Z_EPS = 1e-6
TRI_SCALE = 0.03125

ENGINES = ("tensor", "vector", "scalar", "gpsimd", "sync")

STAGED_W = [
    ("L0_ffn_W1", 4096, 2048),
    ("L0_ffn_W2", 4096, 4096),
    ("L0_ffn_W3", 2048, 4096),
    ("L1_attn_q_w", 2048, 2048),
    ("L1_attn_k_w", 2048, 2048),
    ("L1_attn_v_w", 2048, 2048),
    ("L1_attn_o_w", 2048, 2048),
]
STAGED_8 = [
    ("L1_ffn_gate_w", 6144, 2048),
    ("L1_ffn_up_w", 6144, 2048),
    ("L1_ffn_down_w", 2048, 6144),
]

W_BASES = {}
_p = 0
for _n, _F, _K in STAGED_W:
    W_BASES[_n] = _p
    _p += ((_F // 128) // NCORES) * 128 * _K
SHARD = _p
W8_BASES = {}
_p = 0
for _n, _F, _K in STAGED_8:
    W8_BASES[_n] = _p
    _p += ((_F // 128) // NCORES) * 128 * _K
SHARD8 = _p
WDIMS = {n: (F, K) for n, F, K in STAGED_W}
W8DIMS = {n: (F, K) for n, F, K in STAGED_8}


def staged_loc(name, f):
    F, K = WDIMS[name]
    cpc = (F // 128) // NCORES
    return f // cpc, W_BASES[name] + (f % cpc) * 128 * K, K


def staged8_loc(name, f):
    F, K = W8DIMS[name]
    cpc = (F // 128) // NCORES
    return f // cpc, W8_BASES[name] + (f % cpc) * 128 * K, K


class Ctr:
    def __init__(self, stack, nc, name):
        self.h = stack.enter_context(nc.semaphore(name))
        self.n = 0
        self.name = name

    def inc(self, k=1):
        self.n += k
        return self.n

    def step(self, k=1):
        self.n += k
        return (self.h, k)


class RecInst:
    __slots__ = ("thens",)

    def __init__(self):
        self.thens = []

    def then_inc(self, h, k):
        self.thens.append((h, k))
        return self


class RecEngine:
    def __init__(self, ops):
        self._ops = ops

    def __getattr__(self, name):
        ops = self._ops

        def call(*a, **kw):
            inst = RecInst()
            ops.append((name, a, kw, inst))
            return inst
        return call


class Builder:
    def __init__(self, nc, stack, upto=4):
        self.nc = nc
        self.stack = stack
        self.upto = upto
        self.dram = {}
        self.x_in = nc.dram_tensor("x", [T, D], F32, kind="ExternalInput")
        self.y_out = nc.dram_tensor("y", [T, D], F32, kind="ExternalOutput")
        self.cmask_in = nc.dram_tensor("cmask", [128, 1], F32, kind="ExternalInput")
        self.smask_in = nc.dram_tensor("sendmask", [1, 1], F32, kind="ExternalInput")
        self.stw_in = nc.dram_tensor("stw", [1, SHARD], F32, kind="ExternalInput")
        self.st8_in = nc.dram_tensor("st8", [1, SHARD8], F32, kind="ExternalInput")
        wdefs = [
            ("L0_attn_Wq", [D, D]), ("L0_attn_Wk", [D, D]), ("L0_attn_Wv", [D, D]),
            ("L0_attn_Wo", [D, D]),
        ]
        vdefs = [
            ("L0_ffn_b1", [4096]), ("L0_ffn_b2", [4096]), ("L0_ffn_b3", [D]),
            ("L1_attn_q_b", [D]), ("L1_attn_k_b", [D]), ("L1_attn_v_b", [D]),
            ("L1_attn_o_b", [D]),
            ("L1_ffn_gate_scale", [6144]), ("L1_ffn_up_scale", [6144]),
            ("L1_ffn_down_scale", [D]),
        ]
        for n, s in wdefs + vdefs:
            self.dram[n] = nc.dram_tensor(n, s, F32, kind="ExternalInput")
        self.cc_car_in = nc.dram_tensor("cc_car_in", [1, D], BF16)
        self.cc_car_out = nc.dram_tensor("cc_car_out", [1, D], BF16)
        self.cc_kv_in = [nc.dram_tensor(f"cc_kv_in{i}", [128, 8 * 129], BF16)
                         for i in range(2)]
        self.cc_kv_out = [nc.dram_tensor(f"cc_kv_out{i}", [128, 8 * 129], BF16)
                          for i in range(2)]
        self.ag_in = nc.dram_tensor("ag_in", [1, SHARD], BF16)
        self.ag_out = nc.dram_tensor("ag_out", [NCORES, SHARD], BF16, addr_space="Shared")
        self.ag8_in = nc.dram_tensor("ag8_in", [1, SHARD8], FP8)
        self.ag8_out = nc.dram_tensor("ag8_out", [NCORES, SHARD8], FP8, addr_space="Shared")

        self.XD = Ctr(stack, nc, "XD")
        self.XS = Ctr(stack, nc, "XS")
        self.XT = Ctr(stack, nc, "XT")
        self.SQ = [Ctr(stack, nc, f"SQ{i}") for i in range(4)]
        self.WQ = [Ctr(stack, nc, f"WQ{i}") for i in range(4)]
        self.WS = Ctr(stack, nc, "WS")
        self.TT = Ctr(stack, nc, "TT")
        self.VV = Ctr(stack, nc, "VV")
        self.SS = Ctr(stack, nc, "SS")
        self.GG = Ctr(stack, nc, "GG")
        self.CC = Ctr(stack, nc, "CC")
        self.BAR = {e: Ctr(stack, nc, "BAR_" + e) for e in ENGINES}
        self.ctr_of = {"tensor": self.TT, "vector": self.VV, "scalar": self.SS,
                       "gpsimd": self.GG}
        self._waited = {}

        self.ACC = [stack.enter_context(nc.psum_tensor(f"acc{i}", [128, 1024], F32))
                    for i in range(2)]
        self.TRP = [stack.enter_context(nc.psum_tensor(f"trp{i}", [128, 1024], BF16))
                    for i in range(2)]
        self.AUX = [stack.enter_context(nc.psum_tensor(f"aux{i}", [128, 512], F32))
                    for i in range(2)]
        self.pfree = {}

        self.x_tm = self.sb("x_tm", [128, NTC * D], F32)
        self.tmpbf = self.sb("tmpbf", [128, 1024], BF16)
        self.tri = self.sb("tri", [128, 128], BF16)
        self.identb = self.sb("identb", [128, 128], BF16)
        self.identf = self.sb("identf", [128, 128], F32)
        self.ones_row = self.sb("ones_row", [1, 128], BF16)
        self.ones_col = self.sb("ones_col", [128, 1], BF16)
        self.scol = self.sb("scol", [128, 1], BF16)
        self.cmask = self.sb("cmask_s", [128, 1], F32)
        self.smask = self.sb("smask_s", [1, 1], F32)
        self.craw = self.sb("craw", [16, 128], BF16)
        self.c_cols = self.sb("c_cols", [128, 16], F32)
        self.rawv = self.sb("rawv", [48, 128], F32)
        self.cols = {}
        for n, w in [("b1", 32), ("b2", 32), ("b3", 16), ("qb", 16), ("kb", 16),
                     ("vb", 16), ("ob", 16), ("gs", 48), ("us", 48), ("ds", 16)]:
            self.cols[n] = self.sb("col_" + n, [128, w], F32)
        self.bn24 = self.sb("bn24", [128, 24], F32)
        self.mv = self.sb("mv", [128, 2 * NTC], F32)
        self.sveps = self.sb("sveps", [128, NTC], F32)
        self.ssd = self.sb("ssd", [128, NTC], F32)
        self.ssd2 = self.sb("ssd2", [128, NTC], F32)
        self.rstd = self.sb("rstd", [128, NTC], F32)
        self.nbias = self.sb("nbias", [128, NTC], F32)
        self.zeps = self.sb("zeps", [128, 1], F32)
        self.zrec = self.sb("zrec", [128, 1], F32)
        self.ktmp = self.sb("ktmp", [128, 512], BF16)
        self._tg_i = 0
        self._tg_hist = {}
        self._tags = {}
        self._bg_dmas = []
        self.w8_tag = 0
        self.ag_tags = {}
        self.stage_xt = 0
        self.block = None
        self.debug_dumps = False
        self._dumps = {}
        self._pend = {e: [] for e in ENGINES}
        self._rec = {e: RecEngine(self._pend[e]) for e in ENGINES}

    def sb(self, name, shape, dtype, stack=None):
        return (stack or self.stack).enter_context(self.nc.sbuf_tensor(name, shape, dtype))

    def sec(self, engname, fn):
        fn(self._rec[engname])

    def flush(self):
        for e in ENGINES:
            ops = self._pend[e]
            if not ops:
                continue
            self._pend[e] = []
            self._rec[e] = RecEngine(self._pend[e])

            def runner(eng, ops=ops):
                for name, a, kw, inst in ops:
                    r = getattr(eng, name)(*a, **kw)
                    for (h, k) in inst.thens:
                        r.then_inc(h, k)
            getattr(self.block, e)(runner)

    def wait(self, eng, engname, ctr, val):
        if val is None or val <= 0:
            return
        key = (engname, ctr.name)
        if self._waited.get(key, -1) >= val:
            return
        eng.wait_ge(ctr.h, val)
        self._waited[key] = val

    def barrier(self):
        tags = {e: self.BAR[e].inc() for e in ENGINES}
        for e in ENGINES:
            def mkb(e):
                def f(eng):
                    eng.drain()
                    eng.nop().then_inc(self.BAR[e].h, 1)
                    for o in ENGINES:
                        if o != e:
                            self.wait(eng, e, self.BAR[o], tags[o])
                return f
            self.sec(e, mkb(e))
        self.flush()

    def dump(self, tag, ap, shape):
        if not self.debug_dumps:
            return
        b = self
        dt_ = ap.dtype if hasattr(ap, "dtype") else F32
        d = b.nc.dram_tensor("dump_" + tag, list(shape), dt_, kind="ExternalOutput")
        self._dumps[tag] = d
        self.barrier()
        fin = b.XD.inc(16)

        def s_fn(sync):
            sync.dma_start(out=d.ap(), in_=ap).then_inc(b.XD.h, 16)
            sync.wait_ge(b.XD.h, fin)
        self.sec("sync", s_fn)
        self.barrier()

    def pe_wait_pfree(self, t, key):
        tag = self.pfree.get(key)
        if tag is not None:
            self.wait(t, "tensor", tag[0], tag[1])

    def pump_bg(self):
        self._bg_tick = getattr(self, "_bg_tick", 0) + 1
        if self._bg_dmas and self._bg_tick % 2 == 0:
            self._bg_dmas.pop(0)()

    def drain_bg(self):
        while self._bg_dmas:
            self._bg_dmas.pop(0)()

    def transpose_group(self, src_fn, dst_ap, n, wait=None, tag_key=None,
                        mm_guard=0, ev_guard=None):
        b = self
        g = self._tg_i
        self._tg_i += 1
        trp = b.TRP[g % 2]

        def t_fn(t, g=g, trp=trp):
            if wait is not None:
                self.wait(t, "tensor", wait[0], wait[1])
            h = self._tg_hist.get(g - 2)
            if h:
                self.wait(t, "tensor", h["evctr"], h["evtag"])
            for j in range(n):
                t.transpose(trp[:, j * 128:(j + 1) * 128], src_fn(j),
                            b.identb[:, :]).then_inc(*b.TT.step(1))
        self.sec("tensor", t_fn)
        trtag = b.TT.n
        eng = "vector" if g % 2 == 0 else "scalar"
        ctr = self.ctr_of[eng]

        def e_fn(e, g=g, trp=trp, trtag=trtag, eng=eng, ctr=ctr):
            self.wait(e, eng, b.TT, max(trtag, mm_guard))
            if ev_guard is not None:
                self.wait(e, eng, ev_guard[0], ev_guard[1])
            src = trp[:, 0:n * 128]
            if eng == "vector":
                e.tensor_copy(out=dst_ap, in_=src).then_inc(*ctr.step(1))
            else:
                e.copy(out=dst_ap, in_=src).then_inc(*ctr.step(1))
        self.sec(eng, e_fn)
        self._tg_hist[g] = {"evctr": ctr, "evtag": ctr.n}
        if tag_key is not None:
            self._tags[tag_key] = b.TT.n
        return (ctr, ctr.n)

    def ln_stats(self, src_chunk_ap):
        b = self

        def v_fn(v):
            for c in range(NTC):
                ap = src_chunk_ap(c)
                for k in range(4):
                    ts = v.bn_stats(out=b.bn24[:, 6 * k:6 * k + 6],
                                    in_=ap[:, 512 * k:512 * (k + 1)])
                    if k == 3:
                        ts.then_inc(*b.VV.step(1))
                self.wait(v, "vector", b.VV, b.VV.n)
                v.bn_aggr(out=b.mv[:, 2 * c:2 * c + 2],
                          in_=b.bn24[:, :]).then_inc(*b.VV.step(1))
                self.wait(v, "vector", b.VV, b.VV.n)
            v.tensor_scalar(out=b.sveps[:, :], in0=b.mv[:, 1:2 * NTC:2], scalar1=LN_EPS,
                            scalar2=None, op0=OP.add).then_inc(*b.VV.step(1))
        self.sec("vector", v_fn)
        vtag = b.VV.n

        def s_fn(s):
            self.wait(s, "scalar", b.VV, vtag)
            s.activation(out=b.ssd[:, :], in_=b.sveps[:, :],
                         func=AF.Sqrt).then_inc(*b.SS.step(1))
        self.sec("scalar", s_fn)
        stag = b.SS.n

        def v2_fn(v):
            self.wait(v, "vector", b.SS, stag)
            v.reciprocal(out=b.ssd2[:, :], in_=b.ssd[:, :]).then_inc(*b.VV.step(1))
            self.wait(v, "vector", b.VV, b.VV.n)
            v.tensor_mul(b.ssd2[:, :], b.ssd2[:, :],
                         b.sveps[:, :]).then_inc(*b.VV.step(1))
            self.wait(v, "vector", b.VV, b.VV.n)
            v.tensor_add(b.ssd2[:, :], b.ssd2[:, :],
                         b.ssd[:, :]).then_inc(*b.VV.step(1))
            self.wait(v, "vector", b.VV, b.VV.n)
            v.tensor_scalar(out=b.ssd2[:, :], in0=b.ssd2[:, :], scalar1=0.5,
                            scalar2=None, op0=OP.mult).then_inc(*b.VV.step(1))
            self.wait(v, "vector", b.VV, b.VV.n)
            v.reciprocal(out=b.rstd[:, :], in_=b.ssd2[:, :]).then_inc(*b.VV.step(1))
            self.wait(v, "vector", b.VV, b.VV.n)
            v.scalar_tensor_tensor(out=b.nbias[:, :], in0=b.mv[:, 0:2 * NTC:2],
                                   scalar=-1.0, in1=b.rstd[:, :], op0=OP.mult,
                                   op1=OP.mult).then_inc(*b.VV.step(1))
        self.sec("vector", v2_fn)
        return (b.VV, b.VV.n)

    def tm_to_fm(self, src_chunk_ap, dst_fm, normalize, extra_wait=None):
        b = self
        dst3 = dst_fm[:, :].rearrange("p (i t) -> p i t", i=NDC)
        qctr = self._tags.get("t2f_q", 0)
        for c in range(NTC):
            for q in range(4):
                half = qctr % 2
                tb = b.tmpbf[:, half * 512:(half + 1) * 512]
                prev = self._tags.get(("t2fq", qctr - 2))

                def s_fn(s, c=c, q=q, tb=tb, prev=prev):
                    if extra_wait is not None:
                        self.wait(s, "scalar", extra_wait[0], extra_wait[1])
                    if prev:
                        self.wait(s, "scalar", b.TT, prev)
                    src = src_chunk_ap(c)[:, q * 512:(q + 1) * 512]
                    if normalize:
                        s.activation(out=tb, in_=src, func=AF.Identity,
                                     scale=b.rstd[:, c:c + 1],
                                     bias=b.nbias[:, c:c + 1]).then_inc(*b.SS.step(1))
                    else:
                        s.copy(out=tb, in_=src).then_inc(*b.SS.step(1))
                self.sec("scalar", s_fn)
                vtag = b.SS.n
                self.transpose_group(
                    lambda j, tb=tb: tb[:, j * 128:(j + 1) * 128],
                    dst3[:, q * 4:(q + 1) * 4, c * 128:(c + 1) * 128],
                    4, wait=(b.SS, vtag), tag_key=("t2fq", qctr))
                qctr += 1
        self._tags["t2f_q"] = qctr

    def fm_to_tm(self, src_fm, dst_tm):
        b = self
        for c in range(NTC):
            for g2 in range(2):
                self.transpose_group(
                    lambda j, c=c, g2=g2: src_fm[:, (g2 * 8 + j) * T + c * 128:
                                                 (g2 * 8 + j) * T + (c + 1) * 128],
                    dst_tm[:, c * D + g2 * 1024: c * D + (g2 + 1) * 1024],
                    8)

    def post_ln_residual(self, out_tm, final=False):
        b = self
        self.ln_stats(lambda c: out_tm[:, c * D:(c + 1) * D])
        for c in range(NTC):
            def v_fn(v, c=c):
                xc = b.x_tm[:, c * D:(c + 1) * D]
                oc = out_tm[:, c * D:(c + 1) * D]
                v.scalar_tensor_tensor(out=xc, in0=oc, scalar=b.rstd[:, c:c + 1],
                                       in1=xc, op0=OP.mult,
                                       op1=OP.add).then_inc(*b.VV.step(1))
            self.sec("vector", v_fn)
            vt = b.VV.n

            def s_fn(s, c=c, vt=vt):
                self.wait(s, "scalar", b.VV, vt)
                s.add(out=b.x_tm[:, c * D:(c + 1) * D],
                      in_=b.x_tm[:, c * D:(c + 1) * D],
                      add=b.nbias[:, c:c + 1]).then_inc(*b.SS.step(1))
            self.sec("scalar", s_fn)
            if final:
                st = b.SS.n
                b.XD.inc(16)

                def y_fn(sync, c=c, st=st):
                    self.wait(sync, "sync", b.SS, st)
                    sync.dma_start(
                        out=b.y_out.ap()[c * 128:(c + 1) * 128, :],
                        in_=b.x_tm[:, c * D:(c + 1) * D]).then_inc(b.XD.h, 16)
                self.sec("sync", y_fn)
        if final:
            b.stored_final = True

    def pre_ln(self, dst_fm):
        b = self
        ln_tag = self.ln_stats(lambda c: b.x_tm[:, c * D:(c + 1) * D])
        self.tm_to_fm(lambda c: b.x_tm[:, c * D:(c + 1) * D], dst_fm,
                      normalize=True, extra_wait=ln_tag)

    def stage_weights(self, isc):
        b = self
        sfp = [self.sb(f"stg_f{i}", [128, 4096], BF16, isc) for i in range(2)]
        sout = [self.sb(f"stg_o{i}", [128, 4096], BF16, isc) for i in range(2)]
        sfp_free = [0, 0]
        my_pieces = []
        for name, F, K in STAGED_W:
            cpc = (F // 128) // NCORES
            for lf in range(cpc):
                my_pieces.append((name, K, W_BASES[name] + lf * 128 * K))
        for pi, (name, K, off) in enumerate(my_pieces):
            si = pi % 2
            guard = sfp_free[si]
            sem = b.WQ[si]
            xs = sem.inc(16)

            def ld(g, si=si, K=K, off=off, guard=guard, sem=sem):
                self.wait(g, "gpsimd", b.TT, guard)
                g.dma_start(
                    out=sfp[si][:, 0:K],
                    in_=b.stw_in.ap()[0:1, off:off + 128 * K]
                        .rearrange("o (p k) -> (o p) k", p=128),
                ).then_inc(sem.h, 16)
            self.sec("gpsimd", ld)
            KC = K // 128
            evs = []
            for gq in range((KC + 7) // 8):
                nblk = min(8, KC - gq * 8)
                ev = self.transpose_group(
                    lambda j, si=si, gq=gq: sfp[si][:, (gq * 8 + j) * 128:
                                                    (gq * 8 + j + 1) * 128],
                    sout[si][:, gq * 1024:gq * 1024 + nblk * 128],
                    nblk, wait=(sem, xs), ev_guard=(b.XT, b.XT.n))
                evs.append(ev)
            sfp_free[si] = b.TT.n
            xt = b.XT.inc(16)

            def stdma(sync, si=si, K=K, off=off, evs=tuple(evs)):
                for ev in evs:
                    self.wait(sync, "sync", ev[0], ev[1])
                sync.dma_start(
                    out=b.ag_in.ap()[0:1, off:off + 128 * K]
                        .rearrange("o (p k) -> (o p) k", p=128),
                    in_=sout[si][:, 0:K],
                ).then_inc(b.XT.h, 16)
            self.sec("sync", stdma)
        self.stage_xt = b.XT.n

        for name, F, K in STAGED_8:
            cpc = (F // 128) // NCORES
            for lf in range(cpc):
                off = W8_BASES[name] + lf * 128 * K

                def g_fn(g, off=off, K=K):
                    g.dma_start(
                        out=b.ag8_in.ap()[0:1, off:off + 128 * K]
                            .rearrange("o (p k) -> (o p) k", p=128)
                            .rearrange("p (c j) -> p c j", c=K // 128),
                        in_=b.st8_in.ap()[0:1, off:off + 128 * K]
                            .rearrange("o (c p j) -> (o p) c j", p=128, j=128),
                    ).then_inc(b.WS.h, 16)
                b.WS.inc(16)
                self._bg_dmas.append(lambda fn=g_fn: self.sec("gpsimd", fn))
        self.w8_tag = b.WS.n

    def emit_ag(self, which):
        b = self
        if which == "w":
            xt = self.stage_xt

            def g_fn(g):
                g.wait_ge(b.XT.h, xt)
                g.collective_compute(
                    "AllGather", OP.bypass, replica_groups=[list(range(NCORES))],
                    ins=[b.ag_in.ap()], outs=[b.ag_out.ap()],
                ).then_inc(*b.CC.step(1))
            self.sec("gpsimd", g_fn)
            for name, F, K in STAGED_W:
                self.ag_tags[name] = b.CC.n
        else:
            wl = self.w8_tag

            def g_fn(g):
                g.wait_ge(b.WS.h, wl)
                g.collective_compute(
                    "AllGather", OP.bypass, replica_groups=[list(range(NCORES))],
                    ins=[b.ag8_in.ap()], outs=[b.ag8_out.ap()],
                ).then_inc(*b.CC.step(1))
            self.sec("gpsimd", g_fn)
            for name, F, K in STAGED_8:
                self.ag_tags[name] = b.CC.n

    def init_phase(self):
        b = self

        def sync_fn(sync):
            sync.dma_start(
                out=b.x_tm[:, :].rearrange("p (c d) -> p c d", c=NTC),
                in_=b.x_in.ap().rearrange("(c p) d -> p c d", p=128),
            ).then_inc(b.XD.h, 16)
            sync.dma_start(out=b.cmask[:, :], in_=b.cmask_in[:, :]).then_inc(b.XD.h, 16)
            sync.dma_start(out=b.smask[:, :], in_=b.smask_in[:, :]).then_inc(b.XD.h, 16)
        self.sec("sync", sync_fn)
        b.XD.inc(48)

        def gp_fn(g):
            g.memset(b.tri[:, :], TRI_SCALE).then_inc(*b.GG.step(1))
            g.memset(b.identb[:, :], 1.0).then_inc(*b.GG.step(1))
            g.memset(b.identf[:, :], 1.0).then_inc(*b.GG.step(1))
            g.memset(b.ones_row[:, :], 1.0).then_inc(*b.GG.step(1))
            g.memset(b.ones_col[:, :], 1.0).then_inc(*b.GG.step(1))
            g.memset(b.scol[:, :], TRI_SCALE).then_inc(*b.GG.step(1))
            g.wait_ge(b.GG.h, 6)
            g.affine_select(out=b.tri[:, :], in_=b.tri[:, :], pattern=[[1, 128]],
                            compare_op=OP.is_ge, fill=0.0, base=0,
                            channel_multiplier=-1).then_inc(*b.GG.step(1))
            g.affine_select(out=b.identb[:, :], in_=b.identb[:, :], pattern=[[1, 128]],
                            compare_op=OP.is_equal, fill=0.0, base=0,
                            channel_multiplier=-1).then_inc(*b.GG.step(1))
            g.affine_select(out=b.identf[:, :], in_=b.identf[:, :], pattern=[[1, 128]],
                            compare_op=OP.is_equal, fill=0.0, base=0,
                            channel_multiplier=-1).then_inc(*b.GG.step(1))
        self.sec("gpsimd", gp_fn)
        init_gg = b.GG.n

        vecs = [("L0_ffn_b1", "b1", 32), ("L0_ffn_b2", "b2", 32), ("L0_ffn_b3", "b3", 16),
                ("L1_attn_q_b", "qb", 16), ("L1_attn_k_b", "kb", 16),
                ("L1_attn_v_b", "vb", 16), ("L1_attn_o_b", "ob", 16),
                ("L1_ffn_gate_scale", "gs", 48), ("L1_ffn_up_scale", "us", 48),
                ("L1_ffn_down_scale", "ds", 16)]
        rawv_tt = 0
        aux_free = None
        for dname, cname, w in vecs:
            xt = b.XD.inc(16)

            def s_fn(sync, dname=dname, w=w, rawv_tt=rawv_tt):
                self.wait(sync, "sync", b.TT, rawv_tt)
                sync.dma_start(
                    out=b.rawv[0:w, :],
                    in_=b.dram[dname].ap().rearrange("(c p) -> c p", p=128),
                ).then_inc(b.XD.h, 16)
            self.sec("sync", s_fn)

            def t_fn(t, w=w, xt=xt, aux_free=aux_free):
                self.wait(t, "tensor", b.XD, xt)
                self.wait(t, "tensor", b.GG, init_gg)
                if aux_free is not None:
                    self.wait(t, "tensor", aux_free[0], aux_free[1])
                t.transpose(b.AUX[0][:, 0:w], b.rawv[0:w, :],
                            b.identf[0:w, 0:w]).then_inc(*b.TT.step(1))
            self.sec("tensor", t_fn)
            rawv_tt = b.TT.n
            tt = b.TT.n

            def v_fn(v, cname=cname, w=w, tt=tt):
                self.wait(v, "vector", b.TT, tt)
                v.tensor_copy(out=b.cols[cname][:, :],
                              in_=b.AUX[0][:, 0:w]).then_inc(*b.VV.step(1))
            self.sec("vector", v_fn)
            aux_free = (b.VV, b.VV.n)
        self.pfree["AUX0"] = aux_free

        xd_all = b.XD.n

        def vw(v):
            self.wait(v, "vector", b.XD, xd_all)
        self.sec("vector", vw)

    def ws_pass(self, streams, FC, KC, rhs_fn, on_f, t_width=1024, on_f_delay=0):
        b = self
        KP = (KC + 15) // 16
        NU = FC * KP
        ntile = t_width // 512
        any_rowT = any(st["kind"] == "rowT" for st in streams)
        assert all((st["kind"] == "rowT") == any_rowT for st in streams)
        for sidx, st in enumerate(streams):
            st["sidx"] = sidx
            st["utag"] = {}
            st["uuser"] = {}
            st["mmtag"] = {}
            st["wst_free"] = {}
            st["ev_by_u"] = {}

        def unit_load(st, u):
            f, p = divmod(u, KP)
            si = u % 2
            kw = min(16, KC - p * 16)
            prev_user = st["uuser"].get(u - 2, 0)
            kind = st["kind"]
            if kind == "rowT":
                sem = b.WQ[2 * st["sidx"] + si]
                wl_after = sem.inc(16)

                def g_fn(g, st=st, f=f, p=p, si=si, kw=kw, prev_user=prev_user,
                         sem=sem):
                    self.wait(g, "gpsimd", b.TT, prev_user)
                    g.dma_start(
                        out=st["slabs"][si][:, 0:kw * 128],
                        in_=st["w"].ap()[f * 128:(f + 1) * 128,
                                         p * 2048:p * 2048 + kw * 128],
                    ).then_inc(sem.h, 16)
                self.sec("gpsimd", g_fn)
            elif kind == "staged":
                sem = b.SQ[2 * st["sidx"] + si]
                wl_after = sem.inc(16)
                r, off, K = staged_loc(st["w"], f)
                cc_tag = self.ag_tags[st["w"]]

                def g_fn(sync, st=st, si=si, kw=kw, prev_user=prev_user, sem=sem,
                         r=r, off=off, K=K, p=p, cc_tag=cc_tag):
                    self.wait(sync, "sync", b.CC, cc_tag)
                    self.wait(sync, "sync", b.TT, prev_user)
                    src = (b.ag_out.ap()[r:r + 1, off:off + 128 * K]
                           .rearrange("o (q k) -> (o q) k", q=128)
                           [:, p * 2048:p * 2048 + kw * 128])
                    sync.dma_start(out=st["slabs"][si][:, 0:kw * 128],
                                   in_=src).then_inc(sem.h, 16)
                self.sec("sync", g_fn)
            else:
                sem = b.WQ[2 * st["sidx"] + si]
                wl_after = sem.inc(16)
                r, off, K = staged8_loc(st["w"], f)
                cc_tag = self.ag_tags[st["w"]]

                def g_fn(g, st=st, si=si, kw=kw, prev_user=prev_user, sem=sem,
                         r=r, off=off, K=K, p=p, cc_tag=cc_tag):
                    self.wait(g, "gpsimd", b.CC, cc_tag)
                    self.wait(g, "gpsimd", b.TT, prev_user)
                    src = (b.ag8_out.ap()[r:r + 1, off:off + 128 * K]
                           .rearrange("o (q k) -> (o q) k", q=128)
                           [:, p * 2048:p * 2048 + kw * 128])
                    g.dma_start(out=st["slabs"][si][:, 0:kw * 128],
                                in_=src).then_inc(sem.h, 16)
                self.sec("gpsimd", g_fn)
            st["utag"][u] = (sem, wl_after)
            self.pump_bg()

        def emit_T(st, u):
            f, p = divmod(u, KP)
            si = u % 2
            kw = min(16, KC - p * 16)
            sem, wl = st["utag"][u]
            ws = st["wstage"][u % 2]
            guard = st["wst_free"].get(u - 2, 0)
            for g2 in range((kw + 7) // 8):
                nblk = min(8, kw - g2 * 8)
                self.transpose_group(
                    lambda j, st=st, si=si, g2=g2: st["slabs"][si][:, (g2 * 8 + j) * 128:
                                                                   (g2 * 8 + j + 1) * 128],
                    ws[:, g2 * 1024:g2 * 1024 + nblk * 128],
                    nblk, wait=(sem, wl), mm_guard=guard)
            ngroups = (kw + 7) // 8
            st["ev_by_u"][u] = [self._tg_hist[self._tg_i - 1 - gg]
                                for gg in range(ngroups)]
            st["uuser"][u] = b.TT.n

        def emit_M(st, u, psum, pkey):
            f, p = divmod(u, KP)
            si = u % 2
            kw = min(16, KC - p * 16)
            if st["kind"] == "rowT":
                lhs_waits = [(ev["evctr"], ev["evtag"]) for ev in st["ev_by_u"][u]]
                sem_wl = None
                slab = st["wstage"][u % 2]
            else:
                lhs_waits = None
                sem_wl = st["utag"][u]
                slab = st["slabs"][si]

            def m_fn(t, st=st, u=u, p=p, kw=kw, slab=slab, lhs_waits=lhs_waits,
                     sem_wl=sem_wl, psum=psum, pkey=pkey):
                if lhs_waits is not None:
                    for lw in lhs_waits:
                        self.wait(t, "tensor", lw[0], lw[1])
                else:
                    self.wait(t, "tensor", sem_wl[0], sem_wl[1])
                for dd in range(kw):
                    d = p * 16 + dd
                    if d == 0:
                        self.pe_wait_pfree(t, pkey)
                    rhs = rhs_fn(d)
                    lhsT = slab[:, dd * 128:(dd + 1) * 128]
                    for k in range(ntile):
                        mm = t.matmul(psum[:, k * 512:(k + 1) * 512], lhsT,
                                      rhs[:, k * 512:(k + 1) * 512],
                                      start=(d == 0), stop=(d == KC - 1))
                        if dd == kw - 1 and k == ntile - 1:
                            mm.then_inc(*b.TT.step(1))
            self.sec("tensor", m_fn)
            st["mmtag"][u] = b.TT.n
            if st["kind"] == "rowT":
                st["wst_free"][u] = b.TT.n
            else:
                st["uuser"][u] = b.TT.n
            nxt = u + 2
            if nxt < NU and nxt not in st["utag"]:
                unit_load(st, nxt)

        def psum_of(st, f):
            if st.get("alt"):
                return b.ACC[f % 2], "ACC" + str(f % 2)
            if st.get("alt2"):
                return b.ACC[(f + 1) % 2], "ACC" + str((f + 1) % 2)
            return st["psum"], st["pkey"]

        def emit_M_one(st, u):
            f, p = divmod(u, KP)
            psum, pkey = psum_of(st, f)
            emit_M(st, u, psum, pkey)

        pend_f = []

        def finish_unit(u):
            f, p = divmod(u, KP)
            if p == KP - 1:
                pend_f.append((f, [psum_of(st, f)[0] for st in streams],
                               [st["mmtag"][u] for st in streams]))
                while len(pend_f) > on_f_delay:
                    on_f(*pend_f.pop(0))

        def flush_f():
            while pend_f:
                on_f(*pend_f.pop(0))

        def emit_unit_mms(u):
            for st in streams:
                emit_M_one(st, u)
            finish_unit(u)

        for st in streams:
            unit_load(st, 0)
            if NU > 1:
                unit_load(st, 1)

        if any_rowT:
            prev = None
            for u in range(NU):
                for st in streams:
                    emit_T(st, u)
                    nxt = u + 2
                    if nxt < NU and nxt not in st["utag"]:
                        unit_load(st, nxt)
                    if prev is not None:
                        emit_M_one(st, prev)
                if prev is not None:
                    finish_unit(prev)
                prev = u
            for st in streams:
                emit_M_one(st, prev)
            finish_unit(prev)
            flush_f()
        else:
            for u in range(NU):
                emit_unit_mms(u)
            flush_f()

    def block_l0_attn(self, sc):
        b = self
        xn = self.sb("l0a_xn", [128, NDC * T], BF16, sc)
        q_fm = self.sb("l0a_q", [128, NDC * T], BF16, sc)
        kv_tm = self.sb("l0a_kvtm", [128, NTC * D], BF16, sc)
        carry = self.sb("l0a_carry", [1, D], BF16, sc)
        b.ktmp2 = q_fm[:, 0:2048]
        b.kvq = q_fm[:, 2048:4096]
        msend = carry
        kvtm3 = kv_tm[:, :].rearrange("p (c d) -> p c d", c=NTC)

        self.pre_ln(xn)
        self.barrier()
        self.dump("xn", xn[:, :], [128, NDC * T])

        with ExitStack() as psc:
            slabs_k = [self.sb(f"l0a_sk{i}", [128, D], BF16, psc) for i in range(2)]
            slabs_v = [self.sb(f"l0a_sv{i}", [128, D], BF16, psc) for i in range(2)]
            wst_k = [self.sb(f"l0a_wk{i}", [128, D], BF16, psc) for i in range(2)]
            wst_v = [self.sb(f"l0a_wv{i}", [128, D], BF16, psc) for i in range(2)]
            streams = [
                {"kind": "rowT", "w": b.dram["L0_attn_Wk"], "slabs": slabs_k,
                 "wstage": wst_k, "psum": b.ACC[0], "pkey": "ACC0"},
                {"kind": "rowT", "w": b.dram["L0_attn_Wv"], "slabs": slabs_v,
                 "wstage": wst_v, "psum": b.ACC[1], "pkey": "ACC1"},
            ]
            def on_f(f, psums, mm_tags):
                prev = self._tags.get(("l0kvq", f - 2))

                def v_fn(v, f=f, prev=prev, mt=max(mm_tags)):
                    self.wait(v, "vector", b.TT, mt)
                    if prev:
                        self.wait(v, "vector", b.TT, prev)
                    v.tensor_copy(out=b.ktmp2[:, (f % 2) * 1024:(f % 2) * 1024 + 1024],
                                  in_=b.ACC[0][:, :]).then_inc(*b.VV.step(1))
                    self.wait(v, "vector", b.VV, b.VV.n)
                    v.tensor_tensor(out=b.kvq[:, (f % 2) * 1024:(f % 2) * 1024 + 1024],
                                    in0=b.ktmp2[:, (f % 2) * 1024:(f % 2) * 1024 + 1024],
                                    in1=b.ACC[1][:, :],
                                    op=OP.mult).then_inc(*b.VV.step(1))
                self.sec("vector", v_fn)
                self.pfree["ACC0"] = (b.VV, b.VV.n - 1)
                self.pfree["ACC1"] = (b.VV, b.VV.n)
                vtag = b.VV.n
                self.transpose_group(
                    lambda j, f=f: b.kvq[:, (f % 2) * 1024 + j * 128:
                                         (f % 2) * 1024 + (j + 1) * 128],
                    kvtm3[:, 0:8, f * 128:(f + 1) * 128],
                    8, wait=(b.VV, vtag), tag_key=("l0kvq", f))

            self.ws_pass(streams, NDC, NDC,
                         lambda d: xn[:, d * T:(d + 1) * T], on_f)
        self.barrier()
        self.dump("kvtm", kv_tm[:, :], [128, NTC * D])

        slots = [(b.ACC[0], 0), (b.ACC[0], 512), (b.ACC[1], 0), (b.ACC[1], 512)]

        def t_tot(t):
            for k in range(4):
                ps, off = slots[k]
                for c in range(NTC):
                    mm = t.matmul(ps[0:1, off:off + 512], b.scol[:, 0:1],
                                  kv_tm[:, c * D + k * 512: c * D + (k + 1) * 512],
                                  start=(c == 0), stop=(c == NTC - 1))
                    if c == NTC - 1:
                        mm.then_inc(*b.TT.step(1))
        self.sec("tensor", t_tot)
        tot_mm = b.TT.n

        def v_tot(v):
            self.wait(v, "vector", b.TT, tot_mm)
            for k in range(4):
                ps, off = slots[k]
                v.tensor_single_scalar(out=msend[0:1, k * 512:(k + 1) * 512],
                                       in_=ps[0:1, off:off + 512],
                                       scalar=b.smask[0:1, 0:1],
                                       op=OP.mult)
            v.drain()
            v.nop().then_inc(*b.VV.step(1))
        self.sec("vector", v_tot)
        self.pfree["ACC0"] = (b.VV, b.VV.n)
        self.pfree["ACC1"] = (b.VV, b.VV.n)
        vs = b.VV.n
        xs = b.XD.inc(16)

        def s_out(sync):
            self.wait(sync, "sync", b.VV, vs)
            sync.dma_start(out=b.cc_car_in[:, :], in_=msend[0:1, :]).then_inc(b.XD.h, 16)
        self.sec("sync", s_out)

        def g_cc(g):
            g.wait_ge(b.XD.h, xs)
            g.collective_compute(
                "AllReduce", OP.add, replica_groups=[[0, 1], [2, 3], [4, 5], [6, 7]],
                ins=[b.cc_car_in.ap()], outs=[b.cc_car_out.ap()],
            ).then_inc(*b.CC.step(1))
        self.sec("gpsimd", g_cc)
        cct = b.CC.n
        if self.upto >= 2:
            self.emit_ag("w")
        if self.upto >= 4:
            self.drain_bg()
            self.emit_ag("f8")

        with ExitStack() as psc:
            slabs_q = [self.sb(f"l0a_sq{i}", [128, D], BF16, psc) for i in range(2)]
            wst_q = [self.sb(f"l0a_wq{i}", [128, D], BF16, psc) for i in range(2)]
            st = {"kind": "rowT", "w": b.dram["L0_attn_Wq"], "slabs": slabs_q,
                  "wstage": wst_q, "alt": True}

            def on_f(f, psums, mm_tags):
                def s_fn(s, f=f, ps=psums[0], mt=mm_tags[0]):
                    self.wait(s, "scalar", b.TT, mt)
                    s.copy(out=q_fm[:, f * T:(f + 1) * T],
                           in_=ps[:, :]).then_inc(*b.SS.step(1))
                self.sec("scalar", s_fn)
                self.pfree["ACC" + str(f % 2)] = (b.SS, b.SS.n)

            self.ws_pass([st], NDC, NDC, lambda d: xn[:, d * T:(d + 1) * T], on_f)
        self.barrier()
        self.dump("q", q_fm[:, :], [128, NDC * T])

        kvt_fm = xn
        kvt3 = kvt_fm[:, :].rearrange("p (i t) -> p i t", i=NDC)
        slot_free = [None, None, None, None]
        car_tags = {}
        aux_cs_free = [self.pfree.get("AUX0"), self.pfree.get("AUX1")]
        qc_cs = [0]
        for c in range(NTC):
            for k in range(4):
                ps, off = slots[k]
                sf = slot_free[k]
                ct = car_tags.get((c - 1, k))
                aux = b.AUX[(c * 4 + k) % 2]
                auxf = aux_cs_free[(c * 4 + k) % 2]

                def t_fn(t, c=c, k=k, ps=ps, off=off, sf=sf, ct=ct, aux=aux,
                         auxf=auxf):
                    if sf is not None:
                        self.wait(t, "tensor", sf[0], sf[1])
                    if ct is not None:
                        self.wait(t, "tensor", b.VV, ct)
                    if c == 0:
                        t.matmul(ps[:, off:off + 512], b.tri[:, :],
                                 kv_tm[:, c * D + k * 512: c * D + (k + 1) * 512],
                                 start=True, stop=True).then_inc(*b.TT.step(1))
                    else:
                        t.matmul(ps[:, off:off + 512], b.tri[:, :],
                                 kv_tm[:, c * D + k * 512: c * D + (k + 1) * 512],
                                 start=True, stop=False)
                        t.matmul(ps[:, off:off + 512], b.ones_row[:, :],
                                 carry[0:1, k * 512:(k + 1) * 512],
                                 start=False, stop=True).then_inc(*b.TT.step(1))
                    if c < NTC - 1:
                        if auxf is not None:
                            self.wait(t, "tensor", auxf[0], auxf[1])
                        t.matmul(aux[0:1, 0:512], b.scol[:, 0:1],
                                 kv_tm[:, c * D + k * 512: c * D + (k + 1) * 512],
                                 start=True, stop=True).then_inc(*b.TT.step(1))
                self.sec("tensor", t_fn)
                has_tot = c < NTC - 1
                mtag = b.TT.n - (1 if has_tot else 0)
                tot_tag = b.TT.n
                qc = qc_cs[0]
                qc_cs[0] += 1
                half = qc % 2
                tb = b.tmpbf[:, half * 512:(half + 1) * 512]
                prev = self._tags.get(("l0csq", qc - 2))

                def v_fn(v, c=c, k=k, ps=ps, off=off, mtag=mtag, tot_tag=tot_tag,
                         tb=tb, prev=prev, aux=aux, has_tot=has_tot, xs=xs):
                    self.wait(v, "vector", b.XD, xs)
                    self.wait(v, "vector", b.TT, tot_tag)
                    if prev:
                        self.wait(v, "vector", b.TT, prev)
                    if has_tot:
                        if c == 0:
                            v.tensor_copy(out=carry[0:1, k * 512:(k + 1) * 512],
                                          in_=aux[0:1, 0:512]).then_inc(*b.VV.step(1))
                        else:
                            v.tensor_tensor(out=carry[0:1, k * 512:(k + 1) * 512],
                                            in0=carry[0:1, k * 512:(k + 1) * 512],
                                            in1=aux[0:1, 0:512],
                                            op=OP.add).then_inc(*b.VV.step(1))
                    else:
                        v.nop().then_inc(*b.VV.step(1))
                    v.tensor_copy(out=tb, in_=ps[:, off:off + 512]).then_inc(*b.VV.step(1))
                self.sec("vector", v_fn)
                car_tags[(c, k)] = b.VV.n - 1
                slot_free[k] = (b.VV, b.VV.n)
                aux_cs_free[(c * 4 + k) % 2] = (b.VV, b.VV.n - 1)
                vtag = b.VV.n
                self.transpose_group(
                    lambda j, tb=tb: tb[:, j * 128:(j + 1) * 128],
                    kvt3[:, k * 4:(k + 1) * 4, c * 128:(c + 1) * 128],
                    4, wait=(b.VV, vtag), tag_key=("l0csq", qc))
        self.pfree["AUX0"] = aux_cs_free[0]
        self.pfree["AUX1"] = aux_cs_free[1]
        self.barrier()
        self.dump("kvt", kvt_fm[:, :], [128, NDC * T])

        xr = b.XD.inc(16)

        def s_in(sync):
            self.wait(sync, "sync", b.CC, cct)
            sync.dma_start(out=b.craw[:, :],
                           in_=b.cc_car_out.ap().rearrange("o (c p) -> (o c) p", p=128)
                           ).then_inc(b.XD.h, 16)
        self.sec("sync", s_in)

        g = self._tg_i
        self._tg_i += 1
        trp = b.TRP[g % 2]

        def t_cc(t, g=g, trp=trp):
            self.wait(t, "tensor", b.XD, xr)
            h = self._tg_hist.get(g - 2)
            if h:
                self.wait(t, "tensor", h["evctr"], h["evtag"])
            t.transpose(trp[:, 0:16], b.craw[:, :],
                        b.identb[0:16, 0:16]).then_inc(*b.TT.step(1))
        self.sec("tensor", t_cc)
        tcc = b.TT.n

        def v_cc(v, trp=trp, tcc=tcc):
            self.wait(v, "vector", b.TT, tcc)
            v.tensor_scalar(out=b.c_cols[:, :], in0=trp[:, 0:16],
                            scalar1=b.cmask[:, 0:1], scalar2=None,
                            op0=OP.mult).then_inc(*b.VV.step(1))
        self.sec("vector", v_cc)
        self._tg_hist[g] = {"evctr": b.VV, "evtag": b.VV.n}
        ccols_tag = b.VV.n
        self.dump("ccols", b.c_cols[:, :], [128, 16])

        for i in range(NDC):
            def v_fn(v, i=i):
                self.wait(v, "vector", b.VV, ccols_tag)
                v.scalar_tensor_tensor(out=q_fm[:, i * T:(i + 1) * T],
                                       in0=kvt_fm[:, i * T:(i + 1) * T],
                                       scalar=b.c_cols[:, i:i + 1],
                                       in1=q_fm[:, i * T:(i + 1) * T],
                                       op0=OP.add, op1=OP.mult).then_inc(*b.VV.step(1))
            self.sec("vector", v_fn)
        self.barrier()
        self.dump("vhat", q_fm[:, :], [128, NDC * T])

        out_fm = kv_tm
        with ExitStack() as psc:
            slabs_o = [self.sb(f"l0a_so{i}", [128, D], BF16, psc) for i in range(2)]
            wst_o = [self.sb(f"l0a_wo{i}", [128, D], BF16, psc) for i in range(2)]
            st = {"kind": "rowT", "w": b.dram["L0_attn_Wo"], "slabs": slabs_o,
                  "wstage": wst_o, "alt": True}

            def on_f(f, psums, mm_tags):
                def s_fn(s, f=f, ps=psums[0], mt=mm_tags[0]):
                    self.wait(s, "scalar", b.TT, mt)
                    s.copy(out=out_fm[:, f * T:(f + 1) * T],
                           in_=ps[:, :]).then_inc(*b.SS.step(1))
                self.sec("scalar", s_fn)
                self.pfree["ACC" + str(f % 2)] = (b.SS, b.SS.n)

            self.ws_pass([st], NDC, NDC, lambda d: q_fm[:, d * T:(d + 1) * T], on_f)
        self.barrier()
        self.dump("outfm", out_fm[:, :], [128, NDC * T])

        out_tm = q_fm
        self.fm_to_tm(out_fm, out_tm)
        self.barrier()
        self.post_ln_residual(out_tm)
        self.barrier()

    def block_l0_ffn(self, sc):
        b = self
        xn = self.sb("l0f_xn", [128, NDC * T], BF16, sc)
        h1 = self.sb("l0f_h1", [128, 32 * T], BF16, sc)
        h2b = self.sb("l0f_h2b", [128, 16 * T], BF16, sc)

        self.pre_ln(xn)
        self.barrier()

        def h2_chunk(j):
            return (xn[:, j * T:(j + 1) * T] if j < 16
                    else h2b[:, (j - 16) * T:(j - 15) * T])

        with ExitStack() as psc:
            slabs = [self.sb(f"l0f_s1{i}", [128, D], BF16, psc) for i in range(2)]
            st = {"kind": "staged", "w": "L0_ffn_W1", "slabs": slabs, "alt": True}

            def on_f(f, psums, mm_tags):
                def s_fn(s, f=f, ps=psums[0], mt=mm_tags[0]):
                    self.wait(s, "scalar", b.TT, mt)
                    s.activation(out=h1[:, f * T:(f + 1) * T], in_=ps[:, :],
                                 func=AF.Relu, bias=b.cols["b1"][:, f:f + 1],
                                 scale=1.0).then_inc(*b.SS.step(1))
                self.sec("scalar", s_fn)
                self.pfree["ACC" + str(f % 2)] = (b.SS, b.SS.n)

            self.ws_pass([st], 32, NDC, lambda d: xn[:, d * T:(d + 1) * T], on_f)
        self.barrier()

        with ExitStack() as psc:
            slabs = [self.sb(f"l0f_s2{i}", [128, D], BF16, psc) for i in range(2)]
            st = {"kind": "staged", "w": "L0_ffn_W2", "slabs": slabs, "alt": True}

            def on_f(f, psums, mm_tags):
                def s_fn(s, f=f, ps=psums[0], mt=mm_tags[0]):
                    self.wait(s, "scalar", b.TT, mt)
                    s.activation(out=h2_chunk(f), in_=ps[:, :],
                                 func=AF.Relu, bias=b.cols["b2"][:, f:f + 1],
                                 scale=1.0).then_inc(*b.SS.step(1))
                self.sec("scalar", s_fn)
                self.pfree["ACC" + str(f % 2)] = (b.SS, b.SS.n)

            self.ws_pass([st], 32, 32, lambda d: h1[:, d * T:(d + 1) * T], on_f)
        self.barrier()

        out_fm = h1
        with ExitStack() as psc:
            slabs = [self.sb(f"l0f_s3{i}", [128, D], BF16, psc) for i in range(2)]
            st = {"kind": "staged", "w": "L0_ffn_W3", "slabs": slabs, "alt": True}

            def on_f(f, psums, mm_tags):
                def v_fn(v, f=f, ps=psums[0], mt=mm_tags[0]):
                    self.wait(v, "vector", b.TT, mt)
                    v.tensor_scalar(out=out_fm[:, f * T:(f + 1) * T], in0=ps[:, :],
                                    scalar1=b.cols["b3"][:, f:f + 1], scalar2=None,
                                    op0=OP.add).then_inc(*b.VV.step(1))
                self.sec("vector", v_fn)
                self.pfree["ACC" + str(f % 2)] = (b.VV, b.VV.n)

            self.ws_pass([st], NDC, 32, h2_chunk, on_f)
        self.barrier()

        out_tm = h1[:, 16 * T:32 * T]
        self.fm_to_tm(out_fm, out_tm)
        self.barrier()
        self.post_ln_residual(out_tm)
        self.barrier()

    def block_l1_attn(self, sc):
        b = self
        xn = self.sb("l1a_xn", [128, NDC * T], BF16, sc)
        qp_fm = self.sb("l1a_qp", [128, NDC * T], BF16, sc)
        attn_fm = self.sb("l1a_at", [128, NDC * T], BF16, sc)
        kv_cc = self.sb("l1a_kvcc", [128, 16 * 129], BF16, sc)
        tA = self.sb("l1a_tA", [128, T], F32, sc)
        tB = self.sb("l1a_tB", [128, T], F32, sc)
        kptm2 = attn_fm[:, 0:2 * T]
        vtm2 = attn_fm[:, 2 * T:4 * T]
        kbf2 = attn_fm[:, 4 * T:6 * T]
        vbf2 = attn_fm[:, 6 * T:8 * T]
        kraw = self.sb("l1a_kraw", [128, T], BF16, sc)
        tatt = self.sb("l1a_tatt", [128, 128], BF16, sc)
        b.zbuf = self.sb("l1a_zbuf", [128, 16], F32, sc)
        b.tatt2 = self.sb("l1a_tatt2", [128, 768], BF16, sc)

        self.pre_ln(xn)
        self.barrier()

        with ExitStack() as psc:
            slabs_k = [self.sb(f"l1a_sk{i}", [128, D], BF16, psc) for i in range(2)]
            slabs_v = [self.sb(f"l1a_sv{i}", [128, D], BF16, psc) for i in range(2)]
            streams = [
                {"kind": "staged", "w": "L1_attn_k_w", "slabs": slabs_k,
                 "psum": b.ACC[0], "pkey": "ACC0"},
                {"kind": "staged", "w": "L1_attn_v_w", "slabs": slabs_v,
                 "psum": b.ACC[1], "pkey": "ACC1"},
            ]
            head_state = {}

            def emit_back(h):
                st_h = head_state[h]
                ho = (h % 2) * T
                ev1 = self.transpose_group(
                    lambda j, ho=ho: kbf2[:, ho + j * 128:ho + (j + 1) * 128],
                    kptm2[:, ho:ho + T], 8, wait=st_h["kready"])
                ev2 = self.transpose_group(
                    lambda j, ho=ho: vbf2[:, ho + j * 128:ho + (j + 1) * 128],
                    vtm2[:, ho:ho + T], 8, wait=st_h["vready"])
                st_h["ev"] = [ev1, ev2]
                aux = b.AUX[h % 2]

                def t_fn(t, h=h, ho=ho, aux=aux, ev1=ev1, ev2=ev2):
                    self.wait(t, "tensor", ev1[0], ev1[1])
                    self.wait(t, "tensor", ev2[0], ev2[1])
                    self.pe_wait_pfree(t, "AUX" + str(h % 2))
                    for scn in range(NTC):
                        t.matmul(aux[:, 0:128],
                                 kptm2[:, ho + scn * 128:ho + (scn + 1) * 128],
                                 vtm2[:, ho + scn * 128:ho + (scn + 1) * 128],
                                 start=(scn == 0), stop=False)
                        last = (scn == NTC - 1)
                        mm = t.matmul(aux[:, 128:129],
                                      kptm2[:, ho + scn * 128:ho + (scn + 1) * 128],
                                      b.ones_col[:, 0:1], start=False, stop=last)
                        if last:
                            mm.then_inc(*b.TT.step(1))
                self.sec("tensor", t_fn)
                kvmm = b.TT.n
                st_h["kvmm"] = kvmm

                def v3_fn(v, h=h, aux=aux, kvmm=kvmm):
                    self.wait(v, "vector", b.TT, kvmm)
                    v.tensor_copy(out=kv_cc[:, h * 129:(h + 1) * 129],
                                  in_=aux[:, 0:129]).then_inc(*b.VV.step(1))
                self.sec("vector", v3_fn)
                self.pfree["AUX" + str(h % 2)] = (b.VV, b.VV.n)
                st_h["v3"] = b.VV.n

            def on_f(h, psums, mm_tags):
                mt = max(mm_tags)
                prev_exp = head_state.get(h - 1, {}).get("exp", 0)
                prev_v2 = head_state.get(h - 1, {}).get("v2", 0)
                evp = head_state.get(h - 2, {}).get("ev")
                ho = (h % 2) * T

                def se_fn(s, h=h, ho=ho, mt=mt, prev_exp=prev_exp,
                          prev_v2=prev_v2, evp=evp):
                    self.wait(s, "scalar", b.TT, mt)
                    self.wait(s, "scalar", b.SS, prev_exp)
                    self.wait(s, "scalar", b.VV, prev_v2)
                    if evp:
                        for e in evp:
                            self.wait(s, "scalar", e[0], e[1])
                    s.activation(out=kraw[:, :], in_=b.ACC[0][:, :],
                                 func=AF.Copy).then_inc(*b.SS.step(1))
                    s.activation(out=vbf2[:, ho:ho + T], in_=b.ACC[1][:, :],
                                 func=AF.Identity,
                                 bias=b.cols["vb"][:, h:h + 1]).then_inc(*b.SS.step(1))
                self.sec("scalar", se_fn)
                self.pfree["ACC0"] = (b.SS, b.SS.n - 1)
                self.pfree["ACC1"] = (b.SS, b.SS.n)
                sevac = b.SS.n

                def v_fn(v, h=h, sevac=sevac, prev_v2=prev_v2):
                    self.wait(v, "vector", b.VV, prev_v2)
                    self.wait(v, "vector", b.SS, sevac)
                    v.tensor_scalar(out=tA[:, :], in0=kraw[:, :],
                                    scalar1=b.cols["kb"][:, h:h + 1], scalar2=0.0,
                                    op0=OP.add, op1=OP.min).then_inc(*b.VV.step(1))
                    v.tensor_scalar(out=tB[:, :], in0=kraw[:, :],
                                    scalar1=b.cols["kb"][:, h:h + 1], scalar2=0.0,
                                    op0=OP.add, op1=OP.max).then_inc(*b.VV.step(1))
                self.sec("vector", v_fn)
                vfn_done = b.VV.n

                def s_fn(s, vfn_done=vfn_done):
                    self.wait(s, "scalar", b.VV, vfn_done)
                    s.activation(out=kraw[:, :], in_=tA[:, :],
                                 func=AF.Exp).then_inc(*b.SS.step(1))
                self.sec("scalar", s_fn)
                exp_tag = b.SS.n
                stag = b.SS.n

                def v2_fn(v, h=h, ho=ho, stag=stag, vfn_done=vfn_done, evp=evp):
                    self.wait(v, "vector", b.VV, vfn_done)
                    self.wait(v, "vector", b.SS, stag)
                    if evp:
                        self.wait(v, "vector", evp[0][0], evp[0][1])
                    v.tensor_add(kbf2[:, ho:ho + T], kraw[:, :],
                                 tB[:, :]).then_inc(*b.VV.step(1))
                self.sec("vector", v2_fn)
                kready = b.VV.n
                head_state.setdefault(h, {}).update(
                    {"v2": kready, "exp": exp_tag,
                     "kready": (b.VV, kready), "vready": (b.SS, sevac)})
                if h > 0:
                    emit_back(h - 1)
                if h - 1 == 7:
                    kv_exchange(0)

            HB = 8 * 129
            kv_xr = [None, None]

            def kv_exchange(half):
                lo, hi = half * HB, half * HB + HB
                v3t = head_state[half * 8 + 7]["v3"]
                xs = b.XD.inc(16)

                def s_out(sync, v3t=v3t, half=half, lo=lo, hi=hi):
                    self.wait(sync, "sync", b.VV, v3t)
                    sync.dma_start(out=b.cc_kv_in[half][:, :],
                                   in_=kv_cc[:, lo:hi]).then_inc(b.XD.h, 16)
                self.sec("sync", s_out)

                def g_cc(g, xs=xs, half=half):
                    g.wait_ge(b.XD.h, xs)
                    g.collective_compute(
                        "AllReduce", OP.add,
                        replica_groups=[[0, 1], [2, 3], [4, 5], [6, 7]],
                        ins=[b.cc_kv_in[half].ap()], outs=[b.cc_kv_out[half].ap()],
                    ).then_inc(*b.CC.step(1))
                self.sec("gpsimd", g_cc)
                cct = b.CC.n
                xr = b.XD.inc(16)

                def s_in(sync, cct=cct, half=half, lo=lo, hi=hi):
                    self.wait(sync, "sync", b.CC, cct)
                    sync.dma_start(out=kv_cc[:, lo:hi],
                                   in_=b.cc_kv_out[half][:, :]).then_inc(b.XD.h, 16)
                self.sec("sync", s_in)
                kv_xr[half] = xr

            self.ws_pass(streams, NDC, NDC, lambda d: xn[:, d * T:(d + 1) * T], on_f)
            emit_back(NDC - 1)
            kv_exchange(1)
        self.barrier()
        self.dump("l1kvcc", kv_cc[:, :], [128, 16 * 129])
        kv_sum_ready = kv_xr[1]

        with ExitStack() as psc:
            slabs_q = [self.sb(f"l1a_sq{i}", [128, D], BF16, psc) for i in range(2)]
            st = {"kind": "staged", "w": "L1_attn_q_w", "slabs": slabs_q, "alt": True}
            qp_state = {}

            def on_f(f, psums, mm_tags):
                prev_exp = qp_state.get(f - 1, {}).get("exp", 0)
                prev_v2 = qp_state.get(f - 1, {}).get("v2", 0)

                def v_fn(v, f=f, ps=psums[0], mt=mm_tags[0], prev_exp=prev_exp,
                         prev_v2=prev_v2):
                    self.wait(v, "vector", b.VV, prev_v2)
                    self.wait(v, "vector", b.TT, mt)
                    self.wait(v, "vector", b.SS, prev_exp)
                    v.tensor_scalar(out=tA[:, :], in0=ps[:, :],
                                    scalar1=b.cols["qb"][:, f:f + 1], scalar2=0.0,
                                    op0=OP.add, op1=OP.min).then_inc(*b.VV.step(1))
                    v.tensor_scalar(out=tB[:, :], in0=ps[:, :],
                                    scalar1=b.cols["qb"][:, f:f + 1], scalar2=0.0,
                                    op0=OP.add, op1=OP.max).then_inc(*b.VV.step(1))
                self.sec("vector", v_fn)
                self.pfree["ACC" + str(f % 2)] = (b.VV, b.VV.n)
                vfn_done = b.VV.n
                va = b.VV.n - 1

                def s_fn(s, vfn_done=vfn_done):
                    self.wait(s, "scalar", b.VV, vfn_done)
                    s.activation(out=kraw[:, :], in_=tA[:, :],
                                 func=AF.Exp).then_inc(*b.SS.step(1))
                self.sec("scalar", s_fn)
                qp_state[f] = {"exp": b.SS.n}
                stag = b.SS.n

                def v2_fn(v, f=f, stag=stag, vfn_done=vfn_done):
                    self.wait(v, "vector", b.VV, vfn_done)
                    self.wait(v, "vector", b.SS, stag)
                    v.tensor_add(qp_fm[:, f * T:(f + 1) * T], kraw[:, :],
                                 tB[:, :]).then_inc(*b.VV.step(1))
                self.sec("vector", v2_fn)
                qp_state[f]["v2"] = b.VV.n

            self.ws_pass([st], NDC, NDC, lambda d: xn[:, d * T:(d + 1) * T], on_f)
        self.barrier()
        self.dump("l1qp", qp_fm[:, :], [128, NDC * T])
        self.dump("l1kvsum", kv_cc[:, :], [128, 16 * 129])

        banks4 = [(b.AUX[0], 0, "S0"), (b.AUX[1], 0, "S1"),
                  (b.ACC[0], 0, "S2"), (b.ACC[0], 512, "S3")]
        gi = 0
        for h in range(NDC):
            for g0 in range(3):
                n3 = (3, 3, 2)[g0]
                sc0 = (0, 3, 6)[g0]
                ps4, off4, pk = banks4[gi % 4]
                ta = b.tatt2[:, (gi % 2) * 384:(gi % 2) * 384 + 384]
                zb = b.zbuf[:, (gi % 2) * 4:(gi % 2) * 4 + 3]
                zr = b.zbuf[:, 8 + (gi % 2) * 4: 8 + (gi % 2) * 4 + 3]

                def t_fn(t, h=h, sc0=sc0, n3=n3, ps4=ps4, off4=off4, pk=pk):
                    self.wait(t, "tensor", b.XD, kv_sum_ready)
                    self.pe_wait_pfree(t, pk)
                    for j in range(n3):
                        scn = sc0 + j
                        t.matmul(ps4[:, off4 + j * 129: off4 + (j + 1) * 129],
                                 qp_fm[:, h * T + scn * 128: h * T + (scn + 1) * 128],
                                 kv_cc[:, h * 129:(h + 1) * 129],
                                 start=True, stop=True).then_inc(*b.TT.step(1))
                self.sec("tensor", t_fn)
                mtag = b.TT.n
                prev_t = self._tags.get(("attg", gi - 2))

                def v_fn(v, n3=n3, ps4=ps4, off4=off4, mtag=mtag, prev_t=prev_t,
                         ta=ta, zb=zb, zr=zr):
                    self.wait(v, "vector", b.TT, mtag)
                    if prev_t:
                        self.wait(v, "vector", b.TT, prev_t)
                    v.tensor_scalar(out=zb[:, 0:n3],
                                    in0=ps4[:, off4 + 128: off4 + 128 + (n3 - 1) * 129 + 1:129],
                                    scalar1=Z_EPS, scalar2=None,
                                    op0=OP.add).then_inc(*b.VV.step(1))
                    self.wait(v, "vector", b.VV, b.VV.n)
                    v.reciprocal(out=zr[:, 0:n3], in_=zb[:, 0:n3]).then_inc(*b.VV.step(1))
                    self.wait(v, "vector", b.VV, b.VV.n)
                    for j in range(n3):
                        v.tensor_scalar(out=ta[:, j * 128:(j + 1) * 128],
                                        in0=ps4[:, off4 + j * 129: off4 + j * 129 + 128],
                                        scalar1=zr[:, j:j + 1], scalar2=None,
                                        op0=OP.mult).then_inc(*b.VV.step(1))
                self.sec("vector", v_fn)
                self.pfree[pk] = (b.VV, b.VV.n)
                vtag = b.VV.n
                self.transpose_group(
                    lambda j, ta=ta: ta[:, j * 128:(j + 1) * 128],
                    attn_fm[:, h * T + sc0 * 128: h * T + (sc0 + n3) * 128],
                    n3, wait=(b.VV, vtag), tag_key=("attg", gi))
                gi += 1
        self.barrier()
        self.dump("l1attn", attn_fm[:, :], [128, NDC * T])

        out_fm = xn
        with ExitStack() as psc:
            slabs_o = [self.sb(f"l1a_so{i}", [128, D], BF16, psc) for i in range(2)]
            st = {"kind": "staged", "w": "L1_attn_o_w", "slabs": slabs_o, "alt": True}

            def on_f(f, psums, mm_tags):
                def v_fn(v, f=f, ps=psums[0], mt=mm_tags[0]):
                    self.wait(v, "vector", b.TT, mt)
                    v.tensor_scalar(out=out_fm[:, f * T:(f + 1) * T], in0=ps[:, :],
                                    scalar1=b.cols["ob"][:, f:f + 1], scalar2=None,
                                    op0=OP.add).then_inc(*b.VV.step(1))
                self.sec("vector", v_fn)
                self.pfree["ACC" + str(f % 2)] = (b.VV, b.VV.n)

            self.ws_pass([st], NDC, NDC, lambda d: attn_fm[:, d * T:(d + 1) * T], on_f)
        self.barrier()

        out_tm = qp_fm
        self.fm_to_tm(out_fm, out_tm)
        self.barrier()
        self.post_ln_residual(out_tm)
        self.barrier()

    def block_l1_ffn(self, sc):
        b = self
        I6 = 6144
        NIC = I6 // 128
        xn = self.sb("l1f_xn", [128, NDC * T], BF16, sc)
        fuse = self.sb("l1f_fuse", [128, NIC * 512], BF16, sc)
        out_fm = self.sb("l1f_out", [128, NDC * T], BF16, sc)

        self.pre_ln(xn)
        self.barrier()

        for half in range(2):
            toff = half * 512

            def fuse_chunk(j):
                return fuse[:, j * 512:(j + 1) * 512]

            with ExitStack() as psc:
                slabs_g = [self.sb(f"l1f_sg{half}{i}", [128, D], BF16, psc)
                           for i in range(2)]
                slabs_u = [self.sb(f"l1f_su{half}{i}", [128, D], BF16, psc)
                           for i in range(2)]
                streams = [
                    {"kind": "st8", "w": "L1_ffn_gate_w", "slabs": slabs_g,
                     "psum": b.ACC[0], "pkey": "ACC0"},
                    {"kind": "st8", "w": "L1_ffn_up_w", "slabs": slabs_u,
                     "psum": b.ACC[1], "pkey": "ACC1"},
                ]
                gstate = {}

                def on_f(f, psums, mm_tags):
                    mt = max(mm_tags)
                    prev_fuse = gstate.get(f - 1, {}).get("fuse", 0)

                    def s_fn(s, f=f, mt=mt, prev_fuse=prev_fuse):
                        self.wait(s, "scalar", b.TT, mt)
                        self.wait(s, "scalar", b.VV, prev_fuse)
                        s.activation(out=fuse_chunk(f), in_=b.ACC[0][:, 0:512],
                                     func=AF.Silu, scale=b.cols["gs"][:, f:f + 1],
                                     bias=0.0).then_inc(*b.SS.step(1))
                    self.sec("scalar", s_fn)
                    self.pfree["ACC0"] = (b.SS, b.SS.n)
                    stag = b.SS.n

                    def v_fn(v, f=f, stag=stag):
                        self.wait(v, "vector", b.SS, stag)
                        v.scalar_tensor_tensor(out=fuse_chunk(f),
                                               in0=b.ACC[1][:, 0:512],
                                               scalar=b.cols["us"][:, f:f + 1],
                                               in1=fuse_chunk(f), op0=OP.mult,
                                               op1=OP.mult).then_inc(*b.VV.step(1))
                    self.sec("vector", v_fn)
                    gstate[f] = {"fuse": b.VV.n}
                    self.pfree["ACC1"] = (b.VV, b.VV.n)

                self.ws_pass(streams, NIC, NDC,
                             lambda d: xn[:, d * T + toff: d * T + toff + 512],
                             on_f, t_width=512)
            self.barrier()

            with ExitStack() as psc:
                slabs_d = [self.sb(f"l1f_sd{half}{i}", [128, D], BF16, psc)
                           for i in range(2)]
                st = {"kind": "st8", "w": "L1_ffn_down_w", "slabs": slabs_d,
                      "alt": True}

                def on_f(f, psums, mm_tags):
                    def v_fn(v, f=f, ps=psums[0], mt=mm_tags[0]):
                        self.wait(v, "vector", b.TT, mt)
                        v.tensor_scalar(out=out_fm[:, f * T + toff: f * T + toff + 512],
                                        in0=ps[:, 0:512],
                                        scalar1=b.cols["ds"][:, f:f + 1], scalar2=None,
                                        op0=OP.mult).then_inc(*b.VV.step(1))
                    self.sec("vector", v_fn)
                    self.pfree["ACC" + str(f % 2)] = (b.VV, b.VV.n)

                self.ws_pass([st], NDC, NIC, fuse_chunk, on_f, t_width=512)
            self.barrier()

        out_tm = xn
        self.fm_to_tm(out_fm, out_tm)
        self.barrier()
        self.post_ln_residual(out_tm, final=(self.upto == 4))
        self.barrier()

    def finish(self):
        b = self
        self.drain_bg()
        if getattr(b, "stored_final", False):
            fin = b.XD.n

            def s_fn(sync):
                sync.wait_ge(b.XD.h, fin)
            self.sec("sync", s_fn)
        else:
            vtag = b.VV.n
            ftag = b.SS.n
            final = b.XD.inc(16)

            def s_fn(sync):
                self.wait(sync, "sync", b.VV, vtag)
                self.wait(sync, "sync", b.SS, ftag)
                sync.dma_start(
                    out=b.y_out.ap().rearrange("(c p) d -> p c d", p=128),
                    in_=b.x_tm[:, :].rearrange("p (c d) -> p c d", c=NTC),
                ).then_inc(b.XD.h, 16)
                sync.wait_ge(b.XD.h, final)
            self.sec("sync", s_fn)
        self.flush()


def build_program(upto=4, debug_dumps=False):
    nc = bass.Bass("TRN2", target_bir_lowering=False, debug=False)
    with ExitStack() as stack:
        bld = Builder(nc, stack, upto)
        bld.debug_dumps = debug_dumps
        with nc.Block() as block:
            bld.block = block
            with ExitStack() as isc:
                bld.init_phase()
                if upto >= 2:
                    bld.stage_weights(isc)
                bld.barrier()
            if upto >= 1:
                with ExitStack() as sc:
                    bld.block_l0_attn(sc)
            if upto >= 2:
                with ExitStack() as sc:
                    bld.block_l0_ffn(sc)
            if upto >= 3:
                with ExitStack() as sc:
                    bld.block_l1_attn(sc)
            if upto >= 4:
                with ExitStack() as sc:
                    bld.block_l1_ffn(sc)
            bld.finish()
    return nc


def make_in_maps(inputs, upto=4):
    x = np.asarray(inputs["L0_attn_x"])
    keep = ["L0_attn_Wq", "L0_attn_Wk", "L0_attn_Wv", "L0_attn_Wo",
            "L0_ffn_b1", "L0_ffn_b2", "L0_ffn_b3",
            "L1_attn_q_b", "L1_attn_k_b", "L1_attn_v_b", "L1_attn_o_b",
            "L1_ffn_gate_scale", "L1_ffn_up_scale", "L1_ffn_down_scale"]
    in_maps = []
    for c in range(NCORES):
        bidx, half = c // 2, c % 2
        m = {"x": np.ascontiguousarray(x[bidx, half * T:(half + 1) * T, :]),
             "cmask": np.full((128, 1), float(half), np.float32),
             "sendmask": np.full((1, 1), float(1 - half), np.float32)}
        for k in keep:
            m[k] = np.ascontiguousarray(np.asarray(inputs[k], np.float32))
        parts = []
        for name, F, K in STAGED_W:
            W = np.asarray(inputs[name], np.float32)
            cpc = (F // 128) // NCORES
            for lf in range(cpc):
                f = c * cpc + lf
                parts.append(np.ascontiguousarray(W[f * 128:(f + 1) * 128, :]).reshape(-1))
        m["stw"] = (np.concatenate(parts).reshape(1, SHARD) if parts
                    else np.zeros((1, SHARD), np.float32))
        parts8 = []
        for name, F, K in STAGED_8:
            W = np.asarray(inputs[name], np.float32)
            cpc = (F // 128) // NCORES
            for lf in range(cpc):
                f = c * cpc + lf
                parts8.append(np.ascontiguousarray(W[:, f * 128:(f + 1) * 128]).reshape(-1))
        m["st8"] = (np.concatenate(parts8).reshape(1, SHARD8) if parts8
                    else np.zeros((1, SHARD8), np.float32))
        in_maps.append(m)
    return in_maps


def kernel(**inputs):
    nc = build_program()
    in_maps = make_in_maps(inputs)
    res = run_bass_kernel_spmd(nc, in_maps, core_ids=list(range(NCORES)))
    y = np.empty((4, 2 * T, D), np.float32)
    for c in range(NCORES):
        bidx, half = c // 2, c % 2
        y[bidx, half * T:(half + 1) * T, :] = res.results[c]["y"]
    return y


# version 10
# speedup vs baseline: 1.1333x; 1.1333x over previous
import numpy as np
from contextlib import ExitStack
import concourse.bass as bass
import concourse.mybir as mb
from concourse.bass_utils import run_bass_kernel_spmd

F32 = mb.dt.float32
BF16 = mb.dt.bfloat16
FP8 = mb.dt.float8e4
AF = mb.ActivationFunctionType
OP = mb.AluOpType

NCORES = 8
T = 1024
NTC = T // 128
D = 2048
NDC = D // 128
LN_EPS = 1e-5
Z_EPS = 1e-6
TRI_SCALE = 0.03125

ENGINES = ("tensor", "vector", "scalar", "gpsimd", "sync")

STAGED_W = [
    ("L0_ffn_W1", 4096, 2048),
    ("L0_ffn_W2", 4096, 4096),
    ("L0_ffn_W3", 2048, 4096),
    ("L1_attn_q_w", 2048, 2048),
    ("L1_attn_k_w", 2048, 2048),
    ("L1_attn_v_w", 2048, 2048),
    ("L1_attn_o_w", 2048, 2048),
]
STAGED_8 = [
    ("L1_ffn_gate_w", 6144, 2048),
    ("L1_ffn_up_w", 6144, 2048),
    ("L1_ffn_down_w", 2048, 6144),
]

W_BASES = {}
_p = 0
for _n, _F, _K in STAGED_W:
    W_BASES[_n] = _p
    _p += ((_F // 128) // NCORES) * 128 * _K
SHARD = _p
W8_BASES = {}
_p = 0
for _n, _F, _K in STAGED_8:
    W8_BASES[_n] = _p
    _p += ((_F // 128) // NCORES) * 128 * _K
SHARD8 = _p
WDIMS = {n: (F, K) for n, F, K in STAGED_W}
W8DIMS = {n: (F, K) for n, F, K in STAGED_8}


def staged_loc(name, f):
    F, K = WDIMS[name]
    cpc = (F // 128) // NCORES
    return f // cpc, W_BASES[name] + (f % cpc) * 128 * K, K


def staged8_loc(name, f):
    F, K = W8DIMS[name]
    cpc = (F // 128) // NCORES
    return f // cpc, W8_BASES[name] + (f % cpc) * 128 * K, K


class Ctr:
    def __init__(self, stack, nc, name):
        self.h = stack.enter_context(nc.semaphore(name))
        self.n = 0
        self.name = name

    def inc(self, k=1):
        self.n += k
        return self.n

    def step(self, k=1):
        self.n += k
        return (self.h, k)


class RecInst:
    __slots__ = ("thens",)

    def __init__(self):
        self.thens = []

    def then_inc(self, h, k):
        self.thens.append((h, k))
        return self


class RecEngine:
    def __init__(self, ops):
        self._ops = ops

    def __getattr__(self, name):
        ops = self._ops

        def call(*a, **kw):
            inst = RecInst()
            ops.append((name, a, kw, inst))
            return inst
        return call


class Builder:
    def __init__(self, nc, stack, upto=4):
        self.nc = nc
        self.stack = stack
        self.upto = upto
        self.dram = {}
        self.x_in = nc.dram_tensor("x", [T, D], F32, kind="ExternalInput")
        self.y_out = nc.dram_tensor("y", [T, D], F32, kind="ExternalOutput")
        self.cmask_in = nc.dram_tensor("cmask", [128, 1], F32, kind="ExternalInput")
        self.smask_in = nc.dram_tensor("sendmask", [1, 1], F32, kind="ExternalInput")
        self.stw_in = nc.dram_tensor("stw", [1, SHARD], F32, kind="ExternalInput")
        self.st8_in = nc.dram_tensor("st8", [1, SHARD8], F32, kind="ExternalInput")
        wdefs = [
            ("L0_attn_Wq", [D, D]), ("L0_attn_Wk", [D, D]), ("L0_attn_Wv", [D, D]),
            ("L0_attn_Wo", [D, D]),
        ]
        vdefs = [
            ("L0_ffn_b1", [4096]), ("L0_ffn_b2", [4096]), ("L0_ffn_b3", [D]),
            ("L1_attn_q_b", [D]), ("L1_attn_k_b", [D]), ("L1_attn_v_b", [D]),
            ("L1_attn_o_b", [D]),
            ("L1_ffn_gate_scale", [6144]), ("L1_ffn_up_scale", [6144]),
            ("L1_ffn_down_scale", [D]),
        ]
        for n, s in wdefs + vdefs:
            self.dram[n] = nc.dram_tensor(n, s, F32, kind="ExternalInput")
        self.cc_car_in = nc.dram_tensor("cc_car_in", [1, D], BF16)
        self.cc_car_out = nc.dram_tensor("cc_car_out", [1, D], BF16)
        self.cc_kv_in = [nc.dram_tensor(f"cc_kv_in{i}", [128, 8 * 129], BF16)
                         for i in range(2)]
        self.cc_kv_out = [nc.dram_tensor(f"cc_kv_out{i}", [128, 8 * 129], BF16)
                          for i in range(2)]
        self.ag_in = nc.dram_tensor("ag_in", [1, SHARD], BF16)
        self.ag_out = nc.dram_tensor("ag_out", [NCORES, SHARD], BF16, addr_space="Shared")
        self.ag8_in = nc.dram_tensor("ag8_in", [1, SHARD8], FP8)
        self.ag8_out = nc.dram_tensor("ag8_out", [NCORES, SHARD8], FP8, addr_space="Shared")

        self.XD = Ctr(stack, nc, "XD")
        self.XS = Ctr(stack, nc, "XS")
        self.XT = Ctr(stack, nc, "XT")
        self.SQ = [Ctr(stack, nc, f"SQ{i}") for i in range(4)]
        self.WQ = [Ctr(stack, nc, f"WQ{i}") for i in range(4)]
        self.WS = Ctr(stack, nc, "WS")
        self.TT = Ctr(stack, nc, "TT")
        self.VV = Ctr(stack, nc, "VV")
        self.SS = Ctr(stack, nc, "SS")
        self.GG = Ctr(stack, nc, "GG")
        self.CC = Ctr(stack, nc, "CC")
        self.BAR = {e: Ctr(stack, nc, "BAR_" + e) for e in ENGINES}
        self.ctr_of = {"tensor": self.TT, "vector": self.VV, "scalar": self.SS,
                       "gpsimd": self.GG}
        self._waited = {}

        self.ACC = [stack.enter_context(nc.psum_tensor(f"acc{i}", [128, 1024], F32))
                    for i in range(2)]
        self.TRP = [stack.enter_context(nc.psum_tensor(f"trp{i}", [128, 1024], BF16))
                    for i in range(2)]
        self.AUX = [stack.enter_context(nc.psum_tensor(f"aux{i}", [128, 512], F32))
                    for i in range(2)]
        self.pfree = {}

        self.x_tm = self.sb("x_tm", [128, NTC * D], F32)
        self.tmpbf = self.sb("tmpbf", [128, 1024], BF16)
        self.tri = self.sb("tri", [128, 128], BF16)
        self.identb = self.sb("identb", [128, 128], BF16)
        self.identf = self.sb("identf", [128, 128], F32)
        self.ones_row = self.sb("ones_row", [1, 128], BF16)
        self.ones_col = self.sb("ones_col", [128, 1], BF16)
        self.scol = self.sb("scol", [128, 1], BF16)
        self.cmask = self.sb("cmask_s", [128, 1], F32)
        self.smask = self.sb("smask_s", [1, 1], F32)
        self.craw = self.sb("craw", [16, 128], BF16)
        self.c_cols = self.sb("c_cols", [128, 16], F32)
        self.rawv = self.sb("rawv", [48, 128], F32)
        self.cols = {}
        for n, w in [("b1", 32), ("b2", 32), ("b3", 16), ("qb", 16), ("kb", 16),
                     ("vb", 16), ("ob", 16), ("gs", 48), ("us", 48), ("ds", 16)]:
            self.cols[n] = self.sb("col_" + n, [128, w], F32)
        self.bn24 = self.sb("bn24", [128, 24], F32)
        self.mv = self.sb("mv", [128, 2 * NTC], F32)
        self.sveps = self.sb("sveps", [128, NTC], F32)
        self.ssd = self.sb("ssd", [128, NTC], F32)
        self.ssd2 = self.sb("ssd2", [128, NTC], F32)
        self.rstd = self.sb("rstd", [128, NTC], F32)
        self.nbias = self.sb("nbias", [128, NTC], F32)
        self.zeps = self.sb("zeps", [128, 1], F32)
        self.zrec = self.sb("zrec", [128, 1], F32)
        self.ktmp = self.sb("ktmp", [128, 512], BF16)
        self._tg_i = 0
        self._tg_hist = {}
        self._tags = {}
        self._bg_dmas = []
        self.w8_tag = 0
        self.ag_tags = {}
        self.stage_xt = 0
        self.block = None
        self.debug_dumps = False
        self._dumps = {}
        self._pend = {e: [] for e in ENGINES}
        self._rec = {e: RecEngine(self._pend[e]) for e in ENGINES}

    def sb(self, name, shape, dtype, stack=None):
        return (stack or self.stack).enter_context(self.nc.sbuf_tensor(name, shape, dtype))

    def sec(self, engname, fn):
        fn(self._rec[engname])

    def flush(self):
        for e in ENGINES:
            ops = self._pend[e]
            if not ops:
                continue
            self._pend[e] = []
            self._rec[e] = RecEngine(self._pend[e])

            def runner(eng, ops=ops):
                for name, a, kw, inst in ops:
                    r = getattr(eng, name)(*a, **kw)
                    for (h, k) in inst.thens:
                        r.then_inc(h, k)
            getattr(self.block, e)(runner)

    def wait(self, eng, engname, ctr, val):
        if val is None or val <= 0:
            return
        key = (engname, ctr.name)
        if self._waited.get(key, -1) >= val:
            return
        eng.wait_ge(ctr.h, val)
        self._waited[key] = val

    def barrier(self):
        tags = {e: self.BAR[e].inc() for e in ENGINES}
        for e in ENGINES:
            def mkb(e):
                def f(eng):
                    eng.drain()
                    eng.nop().then_inc(self.BAR[e].h, 1)
                    for o in ENGINES:
                        if o != e:
                            self.wait(eng, e, self.BAR[o], tags[o])
                return f
            self.sec(e, mkb(e))
        self.flush()

    def dump(self, tag, ap, shape):
        if not self.debug_dumps:
            return
        b = self
        dt_ = ap.dtype if hasattr(ap, "dtype") else F32
        d = b.nc.dram_tensor("dump_" + tag, list(shape), dt_, kind="ExternalOutput")
        self._dumps[tag] = d
        self.barrier()
        fin = b.XD.inc(16)

        def s_fn(sync):
            sync.dma_start(out=d.ap(), in_=ap).then_inc(b.XD.h, 16)
            sync.wait_ge(b.XD.h, fin)
        self.sec("sync", s_fn)
        self.barrier()

    def pe_wait_pfree(self, t, key):
        tag = self.pfree.get(key)
        if tag is not None:
            self.wait(t, "tensor", tag[0], tag[1])

    def pump_bg(self):
        self._bg_tick = getattr(self, "_bg_tick", 0) + 1
        if self._bg_dmas and self._bg_tick % 2 == 0:
            self._bg_dmas.pop(0)()

    def drain_bg(self):
        while self._bg_dmas:
            self._bg_dmas.pop(0)()

    def transpose_group(self, src_fn, dst_ap, n, wait=None, tag_key=None,
                        mm_guard=0, ev_guard=None):
        b = self
        g = self._tg_i
        self._tg_i += 1
        trp = b.TRP[g % 2]

        def t_fn(t, g=g, trp=trp):
            if wait is not None:
                self.wait(t, "tensor", wait[0], wait[1])
            h = self._tg_hist.get(g - 2)
            if h:
                self.wait(t, "tensor", h["evctr"], h["evtag"])
            for j in range(n):
                t.transpose(trp[:, j * 128:(j + 1) * 128], src_fn(j),
                            b.identb[:, :]).then_inc(*b.TT.step(1))
        self.sec("tensor", t_fn)
        trtag = b.TT.n
        eng = "vector" if g % 2 == 0 else "scalar"
        ctr = self.ctr_of[eng]

        def e_fn(e, g=g, trp=trp, trtag=trtag, eng=eng, ctr=ctr):
            self.wait(e, eng, b.TT, max(trtag, mm_guard))
            if ev_guard is not None:
                self.wait(e, eng, ev_guard[0], ev_guard[1])
            src = trp[:, 0:n * 128]
            if eng == "vector":
                e.tensor_copy(out=dst_ap, in_=src).then_inc(*ctr.step(1))
            else:
                e.copy(out=dst_ap, in_=src).then_inc(*ctr.step(1))
        self.sec(eng, e_fn)
        self._tg_hist[g] = {"evctr": ctr, "evtag": ctr.n}
        if tag_key is not None:
            self._tags[tag_key] = b.TT.n
        return (ctr, ctr.n)

    def ln_stats(self, src_chunk_ap):
        b = self

        def v_fn(v):
            for c in range(NTC):
                ap = src_chunk_ap(c)
                for k in range(4):
                    ts = v.bn_stats(out=b.bn24[:, 6 * k:6 * k + 6],
                                    in_=ap[:, 512 * k:512 * (k + 1)])
                    if k == 3:
                        ts.then_inc(*b.VV.step(1))
                self.wait(v, "vector", b.VV, b.VV.n)
                v.bn_aggr(out=b.mv[:, 2 * c:2 * c + 2],
                          in_=b.bn24[:, :]).then_inc(*b.VV.step(1))
                self.wait(v, "vector", b.VV, b.VV.n)
            v.tensor_scalar(out=b.sveps[:, :], in0=b.mv[:, 1:2 * NTC:2], scalar1=LN_EPS,
                            scalar2=None, op0=OP.add).then_inc(*b.VV.step(1))
        self.sec("vector", v_fn)
        vtag = b.VV.n

        def s_fn(s):
            self.wait(s, "scalar", b.VV, vtag)
            s.activation(out=b.ssd[:, :], in_=b.sveps[:, :],
                         func=AF.Sqrt).then_inc(*b.SS.step(1))
        self.sec("scalar", s_fn)
        stag = b.SS.n

        def v2_fn(v):
            self.wait(v, "vector", b.SS, stag)
            v.reciprocal(out=b.ssd2[:, :], in_=b.ssd[:, :]).then_inc(*b.VV.step(1))
            self.wait(v, "vector", b.VV, b.VV.n)
            v.tensor_mul(b.ssd2[:, :], b.ssd2[:, :],
                         b.sveps[:, :]).then_inc(*b.VV.step(1))
            self.wait(v, "vector", b.VV, b.VV.n)
            v.tensor_add(b.ssd2[:, :], b.ssd2[:, :],
                         b.ssd[:, :]).then_inc(*b.VV.step(1))
            self.wait(v, "vector", b.VV, b.VV.n)
            v.tensor_scalar(out=b.ssd2[:, :], in0=b.ssd2[:, :], scalar1=0.5,
                            scalar2=None, op0=OP.mult).then_inc(*b.VV.step(1))
            self.wait(v, "vector", b.VV, b.VV.n)
            v.reciprocal(out=b.rstd[:, :], in_=b.ssd2[:, :]).then_inc(*b.VV.step(1))
            self.wait(v, "vector", b.VV, b.VV.n)
            v.scalar_tensor_tensor(out=b.nbias[:, :], in0=b.mv[:, 0:2 * NTC:2],
                                   scalar=-1.0, in1=b.rstd[:, :], op0=OP.mult,
                                   op1=OP.mult).then_inc(*b.VV.step(1))
        self.sec("vector", v2_fn)
        return (b.VV, b.VV.n)

    def tm_to_fm(self, src_chunk_ap, dst_fm, normalize, extra_wait=None):
        b = self
        dst3 = dst_fm[:, :].rearrange("p (i t) -> p i t", i=NDC)
        qctr = self._tags.get("t2f_q", 0)
        for c in range(NTC):
            for q in range(4):
                half = qctr % 2
                tb = b.tmpbf[:, half * 512:(half + 1) * 512]
                prev = self._tags.get(("t2fq", qctr - 2))

                def s_fn(s, c=c, q=q, tb=tb, prev=prev):
                    if extra_wait is not None:
                        self.wait(s, "scalar", extra_wait[0], extra_wait[1])
                    if prev:
                        self.wait(s, "scalar", b.TT, prev)
                    src = src_chunk_ap(c)[:, q * 512:(q + 1) * 512]
                    if normalize:
                        s.activation(out=tb, in_=src, func=AF.Identity,
                                     scale=b.rstd[:, c:c + 1],
                                     bias=b.nbias[:, c:c + 1]).then_inc(*b.SS.step(1))
                    else:
                        s.copy(out=tb, in_=src).then_inc(*b.SS.step(1))
                self.sec("scalar", s_fn)
                vtag = b.SS.n
                self.transpose_group(
                    lambda j, tb=tb: tb[:, j * 128:(j + 1) * 128],
                    dst3[:, q * 4:(q + 1) * 4, c * 128:(c + 1) * 128],
                    4, wait=(b.SS, vtag), tag_key=("t2fq", qctr))
                qctr += 1
        self._tags["t2f_q"] = qctr

    def fm_to_tm(self, src_fm, dst_tm):
        b = self
        for c in range(NTC):
            for g2 in range(2):
                self.transpose_group(
                    lambda j, c=c, g2=g2: src_fm[:, (g2 * 8 + j) * T + c * 128:
                                                 (g2 * 8 + j) * T + (c + 1) * 128],
                    dst_tm[:, c * D + g2 * 1024: c * D + (g2 + 1) * 1024],
                    8)

    def post_ln_residual(self, out_tm, final=False):
        b = self
        self.ln_stats(lambda c: out_tm[:, c * D:(c + 1) * D])
        for c in range(NTC):
            def v_fn(v, c=c):
                xc = b.x_tm[:, c * D:(c + 1) * D]
                oc = out_tm[:, c * D:(c + 1) * D]
                v.scalar_tensor_tensor(out=xc, in0=oc, scalar=b.rstd[:, c:c + 1],
                                       in1=xc, op0=OP.mult,
                                       op1=OP.add).then_inc(*b.VV.step(1))
            self.sec("vector", v_fn)
            vt = b.VV.n

            def s_fn(s, c=c, vt=vt):
                self.wait(s, "scalar", b.VV, vt)
                s.add(out=b.x_tm[:, c * D:(c + 1) * D],
                      in_=b.x_tm[:, c * D:(c + 1) * D],
                      add=b.nbias[:, c:c + 1]).then_inc(*b.SS.step(1))
            self.sec("scalar", s_fn)
            if final:
                st = b.SS.n
                b.XD.inc(16)

                def y_fn(sync, c=c, st=st):
                    self.wait(sync, "sync", b.SS, st)
                    sync.dma_start(
                        out=b.y_out.ap()[c * 128:(c + 1) * 128, :],
                        in_=b.x_tm[:, c * D:(c + 1) * D]).then_inc(b.XD.h, 16)
                self.sec("sync", y_fn)
        if final:
            b.stored_final = True

    def pre_ln(self, dst_fm):
        b = self
        ln_tag = self.ln_stats(lambda c: b.x_tm[:, c * D:(c + 1) * D])
        self.tm_to_fm(lambda c: b.x_tm[:, c * D:(c + 1) * D], dst_fm,
                      normalize=True, extra_wait=ln_tag)

    def stage_weights(self, isc):
        b = self
        sfp = [self.sb(f"stg_f{i}", [128, 4096], BF16, isc) for i in range(2)]
        sout = [self.sb(f"stg_o{i}", [128, 4096], BF16, isc) for i in range(2)]
        sfp_free = [0, 0]
        my_pieces = []
        for name, F, K in STAGED_W:
            cpc = (F // 128) // NCORES
            for lf in range(cpc):
                my_pieces.append((name, K, W_BASES[name] + lf * 128 * K))
        for pi, (name, K, off) in enumerate(my_pieces):
            si = pi % 2
            guard = sfp_free[si]
            sem = b.WQ[si]
            xs = sem.inc(16)

            def ld(g, si=si, K=K, off=off, guard=guard, sem=sem):
                self.wait(g, "gpsimd", b.TT, guard)
                g.dma_start(
                    out=sfp[si][:, 0:K],
                    in_=b.stw_in.ap()[0:1, off:off + 128 * K]
                        .rearrange("o (p k) -> (o p) k", p=128),
                ).then_inc(sem.h, 16)
            self.sec("gpsimd", ld)
            KC = K // 128
            evs = []
            for gq in range((KC + 7) // 8):
                nblk = min(8, KC - gq * 8)
                ev = self.transpose_group(
                    lambda j, si=si, gq=gq: sfp[si][:, (gq * 8 + j) * 128:
                                                    (gq * 8 + j + 1) * 128],
                    sout[si][:, gq * 1024:gq * 1024 + nblk * 128],
                    nblk, wait=(sem, xs), ev_guard=(b.XT, b.XT.n))
                evs.append(ev)
            sfp_free[si] = b.TT.n
            xt = b.XT.inc(16)

            def stdma(sync, si=si, K=K, off=off, evs=tuple(evs)):
                for ev in evs:
                    self.wait(sync, "sync", ev[0], ev[1])
                sync.dma_start(
                    out=b.ag_in.ap()[0:1, off:off + 128 * K]
                        .rearrange("o (p k) -> (o p) k", p=128),
                    in_=sout[si][:, 0:K],
                ).then_inc(b.XT.h, 16)
            self.sec("sync", stdma)
        self.stage_xt = b.XT.n

        for name, F, K in STAGED_8:
            cpc = (F // 128) // NCORES
            for lf in range(cpc):
                off = W8_BASES[name] + lf * 128 * K

                def g_fn(g, off=off, K=K):
                    g.dma_start(
                        out=b.ag8_in.ap()[0:1, off:off + 128 * K]
                            .rearrange("o (p k) -> (o p) k", p=128)
                            .rearrange("p (c j) -> p c j", c=K // 128),
                        in_=b.st8_in.ap()[0:1, off:off + 128 * K]
                            .rearrange("o (c p j) -> (o p) c j", p=128, j=128),
                    ).then_inc(b.WS.h, 16)
                b.WS.inc(16)
                self._bg_dmas.append(lambda fn=g_fn: self.sec("gpsimd", fn))
        self.w8_tag = b.WS.n

    def emit_ag(self, which):
        b = self
        if which == "w":
            xt = self.stage_xt

            def g_fn(g):
                g.wait_ge(b.XT.h, xt)
                g.collective_compute(
                    "AllGather", OP.bypass, replica_groups=[list(range(NCORES))],
                    ins=[b.ag_in.ap()], outs=[b.ag_out.ap()],
                ).then_inc(*b.CC.step(1))
            self.sec("gpsimd", g_fn)
            for name, F, K in STAGED_W:
                self.ag_tags[name] = b.CC.n
        else:
            wl = self.w8_tag

            def g_fn(g):
                g.wait_ge(b.WS.h, wl)
                g.collective_compute(
                    "AllGather", OP.bypass, replica_groups=[list(range(NCORES))],
                    ins=[b.ag8_in.ap()], outs=[b.ag8_out.ap()],
                ).then_inc(*b.CC.step(1))
            self.sec("gpsimd", g_fn)
            for name, F, K in STAGED_8:
                self.ag_tags[name] = b.CC.n

    def init_phase(self):
        b = self

        def sync_fn(sync):
            sync.dma_start(
                out=b.x_tm[:, :].rearrange("p (c d) -> p c d", c=NTC),
                in_=b.x_in.ap().rearrange("(c p) d -> p c d", p=128),
            ).then_inc(b.XD.h, 16)
            sync.dma_start(out=b.cmask[:, :], in_=b.cmask_in[:, :]).then_inc(b.XD.h, 16)
            sync.dma_start(out=b.smask[:, :], in_=b.smask_in[:, :]).then_inc(b.XD.h, 16)
        self.sec("sync", sync_fn)
        b.XD.inc(48)

        def gp_fn(g):
            g.memset(b.tri[:, :], TRI_SCALE).then_inc(*b.GG.step(1))
            g.memset(b.identb[:, :], 1.0).then_inc(*b.GG.step(1))
            g.memset(b.identf[:, :], 1.0).then_inc(*b.GG.step(1))
            g.memset(b.ones_row[:, :], 1.0).then_inc(*b.GG.step(1))
            g.memset(b.ones_col[:, :], 1.0).then_inc(*b.GG.step(1))
            g.memset(b.scol[:, :], TRI_SCALE).then_inc(*b.GG.step(1))
            g.wait_ge(b.GG.h, 6)
            g.affine_select(out=b.tri[:, :], in_=b.tri[:, :], pattern=[[1, 128]],
                            compare_op=OP.is_ge, fill=0.0, base=0,
                            channel_multiplier=-1).then_inc(*b.GG.step(1))
            g.affine_select(out=b.identb[:, :], in_=b.identb[:, :], pattern=[[1, 128]],
                            compare_op=OP.is_equal, fill=0.0, base=0,
                            channel_multiplier=-1).then_inc(*b.GG.step(1))
            g.affine_select(out=b.identf[:, :], in_=b.identf[:, :], pattern=[[1, 128]],
                            compare_op=OP.is_equal, fill=0.0, base=0,
                            channel_multiplier=-1).then_inc(*b.GG.step(1))
        self.sec("gpsimd", gp_fn)
        init_gg = b.GG.n

        vecs = [("L0_ffn_b1", "b1", 32), ("L0_ffn_b2", "b2", 32), ("L0_ffn_b3", "b3", 16),
                ("L1_attn_q_b", "qb", 16), ("L1_attn_k_b", "kb", 16),
                ("L1_attn_v_b", "vb", 16), ("L1_attn_o_b", "ob", 16),
                ("L1_ffn_gate_scale", "gs", 48), ("L1_ffn_up_scale", "us", 48),
                ("L1_ffn_down_scale", "ds", 16)]
        rawv_tt = 0
        aux_free = None
        for dname, cname, w in vecs:
            xt = b.XD.inc(16)

            def s_fn(sync, dname=dname, w=w, rawv_tt=rawv_tt):
                self.wait(sync, "sync", b.TT, rawv_tt)
                sync.dma_start(
                    out=b.rawv[0:w, :],
                    in_=b.dram[dname].ap().rearrange("(c p) -> c p", p=128),
                ).then_inc(b.XD.h, 16)
            self.sec("sync", s_fn)

            def t_fn(t, w=w, xt=xt, aux_free=aux_free):
                self.wait(t, "tensor", b.XD, xt)
                self.wait(t, "tensor", b.GG, init_gg)
                if aux_free is not None:
                    self.wait(t, "tensor", aux_free[0], aux_free[1])
                t.transpose(b.AUX[0][:, 0:w], b.rawv[0:w, :],
                            b.identf[0:w, 0:w]).then_inc(*b.TT.step(1))
            self.sec("tensor", t_fn)
            rawv_tt = b.TT.n
            tt = b.TT.n

            def v_fn(v, cname=cname, w=w, tt=tt):
                self.wait(v, "vector", b.TT, tt)
                v.tensor_copy(out=b.cols[cname][:, :],
                              in_=b.AUX[0][:, 0:w]).then_inc(*b.VV.step(1))
            self.sec("vector", v_fn)
            aux_free = (b.VV, b.VV.n)
        self.pfree["AUX0"] = aux_free

        xd_all = b.XD.n

        def vw(v):
            self.wait(v, "vector", b.XD, xd_all)
        self.sec("vector", vw)

    def ws_pass(self, streams, FC, KC, rhs_fn, on_f, t_width=1024, on_f_delay=0):
        b = self
        KP = (KC + 15) // 16
        NU = FC * KP
        ntile = t_width // 512
        any_rowT = any(st["kind"] == "rowT" for st in streams)
        assert all((st["kind"] == "rowT") == any_rowT for st in streams)
        for sidx, st in enumerate(streams):
            st["sidx"] = sidx
            st["utag"] = {}
            st["uuser"] = {}
            st["mmtag"] = {}
            st["wst_free"] = {}
            st["ev_by_u"] = {}

        def unit_load(st, u):
            f, p = divmod(u, KP)
            si = u % 2
            kw = min(16, KC - p * 16)
            prev_user = st["uuser"].get(u - 2, 0)
            kind = st["kind"]
            if kind == "rowT":
                sem = b.WQ[2 * st["sidx"] + si]
                wl_after = sem.inc(16)

                def g_fn(g, st=st, f=f, p=p, si=si, kw=kw, prev_user=prev_user,
                         sem=sem):
                    self.wait(g, "gpsimd", b.TT, prev_user)
                    g.dma_start(
                        out=st["slabs"][si][:, 0:kw * 128],
                        in_=st["w"].ap()[f * 128:(f + 1) * 128,
                                         p * 2048:p * 2048 + kw * 128],
                    ).then_inc(sem.h, 16)
                self.sec("gpsimd", g_fn)
            elif kind == "staged":
                sem = b.SQ[2 * st["sidx"] + si]
                wl_after = sem.inc(16)
                r, off, K = staged_loc(st["w"], f)
                cc_tag = self.ag_tags[st["w"]]

                def g_fn(sync, st=st, si=si, kw=kw, prev_user=prev_user, sem=sem,
                         r=r, off=off, K=K, p=p, cc_tag=cc_tag):
                    self.wait(sync, "sync", b.CC, cc_tag)
                    self.wait(sync, "sync", b.TT, prev_user)
                    src = (b.ag_out.ap()[r:r + 1, off:off + 128 * K]
                           .rearrange("o (q k) -> (o q) k", q=128)
                           [:, p * 2048:p * 2048 + kw * 128])
                    sync.dma_start(out=st["slabs"][si][:, 0:kw * 128],
                                   in_=src).then_inc(sem.h, 16)
                self.sec("sync", g_fn)
            else:
                sem = b.WQ[2 * st["sidx"] + si]
                wl_after = sem.inc(16)
                r, off, K = staged8_loc(st["w"], f)
                cc_tag = self.ag_tags[st["w"]]

                def g_fn(g, st=st, si=si, kw=kw, prev_user=prev_user, sem=sem,
                         r=r, off=off, K=K, p=p, cc_tag=cc_tag):
                    self.wait(g, "gpsimd", b.CC, cc_tag)
                    self.wait(g, "gpsimd", b.TT, prev_user)
                    src = (b.ag8_out.ap()[r:r + 1, off:off + 128 * K]
                           .rearrange("o (q k) -> (o q) k", q=128)
                           [:, p * 2048:p * 2048 + kw * 128])
                    g.dma_start(out=st["slabs"][si][:, 0:kw * 128],
                                in_=src).then_inc(sem.h, 16)
                self.sec("gpsimd", g_fn)
            st["utag"][u] = (sem, wl_after)
            self.pump_bg()

        def emit_T(st, u):
            f, p = divmod(u, KP)
            si = u % 2
            kw = min(16, KC - p * 16)
            sem, wl = st["utag"][u]
            ws = st["wstage"][u % 2]
            guard = st["wst_free"].get(u - 2, 0)
            for g2 in range((kw + 7) // 8):
                nblk = min(8, kw - g2 * 8)
                self.transpose_group(
                    lambda j, st=st, si=si, g2=g2: st["slabs"][si][:, (g2 * 8 + j) * 128:
                                                                   (g2 * 8 + j + 1) * 128],
                    ws[:, g2 * 1024:g2 * 1024 + nblk * 128],
                    nblk, wait=(sem, wl), mm_guard=guard)
            ngroups = (kw + 7) // 8
            st["ev_by_u"][u] = [self._tg_hist[self._tg_i - 1 - gg]
                                for gg in range(ngroups)]
            st["uuser"][u] = b.TT.n

        def emit_M(st, u, psum, pkey):
            f, p = divmod(u, KP)
            si = u % 2
            kw = min(16, KC - p * 16)
            if st["kind"] == "rowT":
                lhs_waits = [(ev["evctr"], ev["evtag"]) for ev in st["ev_by_u"][u]]
                sem_wl = None
                slab = st["wstage"][u % 2]
            else:
                lhs_waits = None
                sem_wl = st["utag"][u]
                slab = st["slabs"][si]

            def m_fn(t, st=st, u=u, p=p, kw=kw, slab=slab, lhs_waits=lhs_waits,
                     sem_wl=sem_wl, psum=psum, pkey=pkey):
                if lhs_waits is not None:
                    for lw in lhs_waits:
                        self.wait(t, "tensor", lw[0], lw[1])
                else:
                    self.wait(t, "tensor", sem_wl[0], sem_wl[1])
                for dd in range(kw):
                    d = p * 16 + dd
                    if d == 0:
                        self.pe_wait_pfree(t, pkey)
                    rhs = rhs_fn(d)
                    lhsT = slab[:, dd * 128:(dd + 1) * 128]
                    for k in range(ntile):
                        mm = t.matmul(psum[:, k * 512:(k + 1) * 512], lhsT,
                                      rhs[:, k * 512:(k + 1) * 512],
                                      start=(d == 0), stop=(d == KC - 1))
                        if dd == kw - 1 and k == ntile - 1:
                            mm.then_inc(*b.TT.step(1))
            self.sec("tensor", m_fn)
            st["mmtag"][u] = b.TT.n
            if st["kind"] == "rowT":
                st["wst_free"][u] = b.TT.n
            else:
                st["uuser"][u] = b.TT.n
            nxt = u + 2
            if nxt < NU and nxt not in st["utag"]:
                unit_load(st, nxt)

        def psum_of(st, f):
            if st.get("alt"):
                return b.ACC[f % 2], "ACC" + str(f % 2)
            if st.get("alt2"):
                return b.ACC[(f + 1) % 2], "ACC" + str((f + 1) % 2)
            return st["psum"], st["pkey"]

        def emit_M_one(st, u):
            f, p = divmod(u, KP)
            psum, pkey = psum_of(st, f)
            emit_M(st, u, psum, pkey)

        pend_f = []

        def finish_unit(u):
            f, p = divmod(u, KP)
            if p == KP - 1:
                pend_f.append((f, [psum_of(st, f)[0] for st in streams],
                               [st["mmtag"][u] for st in streams]))
                while len(pend_f) > on_f_delay:
                    on_f(*pend_f.pop(0))

        def flush_f():
            while pend_f:
                on_f(*pend_f.pop(0))

        def emit_unit_mms(u):
            for st in streams:
                emit_M_one(st, u)
            finish_unit(u)

        for st in streams:
            unit_load(st, 0)
            if NU > 1:
                unit_load(st, 1)

        if any_rowT:
            prev = None
            for u in range(NU):
                for st in streams:
                    emit_T(st, u)
                    nxt = u + 2
                    if nxt < NU and nxt not in st["utag"]:
                        unit_load(st, nxt)
                    if prev is not None:
                        emit_M_one(st, prev)
                if prev is not None:
                    finish_unit(prev)
                prev = u
            for st in streams:
                emit_M_one(st, prev)
            finish_unit(prev)
            flush_f()
        else:
            for u in range(NU):
                emit_unit_mms(u)
            flush_f()

    def block_l0_attn(self, sc):
        b = self
        xn = self.sb("l0a_xn", [128, NDC * T], BF16, sc)
        q_fm = self.sb("l0a_q", [128, NDC * T], BF16, sc)
        kv_tm = self.sb("l0a_kvtm", [128, NTC * D], BF16, sc)
        carry = self.sb("l0a_carry", [1, D], BF16, sc)
        b.ktmp2 = q_fm[:, 0:2048]
        b.kvq = q_fm[:, 2048:4096]
        msend = carry
        kvtm3 = kv_tm[:, :].rearrange("p (c d) -> p c d", c=NTC)

        self.pre_ln(xn)
        self.barrier()
        self.dump("xn", xn[:, :], [128, NDC * T])

        with ExitStack() as psc:
            slabs_k = [self.sb(f"l0a_sk{i}", [128, D], BF16, psc) for i in range(2)]
            slabs_v = [self.sb(f"l0a_sv{i}", [128, D], BF16, psc) for i in range(2)]
            wst_k = [self.sb(f"l0a_wk{i}", [128, D], BF16, psc) for i in range(2)]
            wst_v = [self.sb(f"l0a_wv{i}", [128, D], BF16, psc) for i in range(2)]
            streams = [
                {"kind": "rowT", "w": b.dram["L0_attn_Wk"], "slabs": slabs_k,
                 "wstage": wst_k, "psum": b.ACC[0], "pkey": "ACC0"},
                {"kind": "rowT", "w": b.dram["L0_attn_Wv"], "slabs": slabs_v,
                 "wstage": wst_v, "psum": b.ACC[1], "pkey": "ACC1"},
            ]
            def on_f(f, psums, mm_tags):
                prev = self._tags.get(("l0kvq", f - 2))

                def v_fn(v, f=f, prev=prev, mt=max(mm_tags)):
                    self.wait(v, "vector", b.TT, mt)
                    if prev:
                        self.wait(v, "vector", b.TT, prev)
                    v.tensor_copy(out=b.ktmp2[:, (f % 2) * 1024:(f % 2) * 1024 + 1024],
                                  in_=b.ACC[0][:, :]).then_inc(*b.VV.step(1))
                    self.wait(v, "vector", b.VV, b.VV.n)
                    v.tensor_tensor(out=b.kvq[:, (f % 2) * 1024:(f % 2) * 1024 + 1024],
                                    in0=b.ktmp2[:, (f % 2) * 1024:(f % 2) * 1024 + 1024],
                                    in1=b.ACC[1][:, :],
                                    op=OP.mult).then_inc(*b.VV.step(1))
                self.sec("vector", v_fn)
                self.pfree["ACC0"] = (b.VV, b.VV.n - 1)
                self.pfree["ACC1"] = (b.VV, b.VV.n)
                vtag = b.VV.n
                self.transpose_group(
                    lambda j, f=f: b.kvq[:, (f % 2) * 1024 + j * 128:
                                         (f % 2) * 1024 + (j + 1) * 128],
                    kvtm3[:, 0:8, f * 128:(f + 1) * 128],
                    8, wait=(b.VV, vtag), tag_key=("l0kvq", f))

            self.ws_pass(streams, NDC, NDC,
                         lambda d: xn[:, d * T:(d + 1) * T], on_f)
        self.barrier()
        self.dump("kvtm", kv_tm[:, :], [128, NTC * D])

        slots = [(b.ACC[0], 0), (b.ACC[0], 512), (b.ACC[1], 0), (b.ACC[1], 512)]

        def t_tot(t):
            for k in range(4):
                ps, off = slots[k]
                for c in range(NTC):
                    mm = t.matmul(ps[0:1, off:off + 512], b.scol[:, 0:1],
                                  kv_tm[:, c * D + k * 512: c * D + (k + 1) * 512],
                                  start=(c == 0), stop=(c == NTC - 1))
                    if c == NTC - 1:
                        mm.then_inc(*b.TT.step(1))
        self.sec("tensor", t_tot)
        tot_mm = b.TT.n

        def v_tot(v):
            self.wait(v, "vector", b.TT, tot_mm)
            for k in range(4):
                ps, off = slots[k]
                v.tensor_single_scalar(out=msend[0:1, k * 512:(k + 1) * 512],
                                       in_=ps[0:1, off:off + 512],
                                       scalar=b.smask[0:1, 0:1],
                                       op=OP.mult)
            v.drain()
            v.nop().then_inc(*b.VV.step(1))
        self.sec("vector", v_tot)
        self.pfree["ACC0"] = (b.VV, b.VV.n)
        self.pfree["ACC1"] = (b.VV, b.VV.n)
        vs = b.VV.n
        xs = b.XD.inc(16)

        def s_out(sync):
            self.wait(sync, "sync", b.VV, vs)
            sync.dma_start(out=b.cc_car_in[:, :], in_=msend[0:1, :]).then_inc(b.XD.h, 16)
        self.sec("sync", s_out)

        def g_cc(g):
            g.wait_ge(b.XD.h, xs)
            g.collective_compute(
                "AllReduce", OP.add, replica_groups=[[0, 1], [2, 3], [4, 5], [6, 7]],
                ins=[b.cc_car_in.ap()], outs=[b.cc_car_out.ap()],
            ).then_inc(*b.CC.step(1))
        self.sec("gpsimd", g_cc)
        cct = b.CC.n
        if self.upto >= 4:
            self.drain_bg()
            self.emit_ag("f8")

        with ExitStack() as psc:
            slabs_q = [self.sb(f"l0a_sq{i}", [128, D], BF16, psc) for i in range(2)]
            wst_q = [self.sb(f"l0a_wq{i}", [128, D], BF16, psc) for i in range(2)]
            st = {"kind": "rowT", "w": b.dram["L0_attn_Wq"], "slabs": slabs_q,
                  "wstage": wst_q, "alt": True}

            def on_f(f, psums, mm_tags):
                def s_fn(s, f=f, ps=psums[0], mt=mm_tags[0]):
                    self.wait(s, "scalar", b.TT, mt)
                    s.copy(out=q_fm[:, f * T:(f + 1) * T],
                           in_=ps[:, :]).then_inc(*b.SS.step(1))
                self.sec("scalar", s_fn)
                self.pfree["ACC" + str(f % 2)] = (b.SS, b.SS.n)

            self.ws_pass([st], NDC, NDC, lambda d: xn[:, d * T:(d + 1) * T], on_f)
        self.barrier()
        self.dump("q", q_fm[:, :], [128, NDC * T])

        kvt_fm = xn
        kvt3 = kvt_fm[:, :].rearrange("p (i t) -> p i t", i=NDC)
        slot_free = [None, None, None, None]
        car_tags = {}
        aux_cs_free = [self.pfree.get("AUX0"), self.pfree.get("AUX1")]
        qc_cs = [0]
        for c in range(NTC):
            for k in range(4):
                ps, off = slots[k]
                sf = slot_free[k]
                ct = car_tags.get((c - 1, k))
                aux = b.AUX[(c * 4 + k) % 2]
                auxf = aux_cs_free[(c * 4 + k) % 2]

                def t_fn(t, c=c, k=k, ps=ps, off=off, sf=sf, ct=ct, aux=aux,
                         auxf=auxf):
                    if sf is not None:
                        self.wait(t, "tensor", sf[0], sf[1])
                    if ct is not None:
                        self.wait(t, "tensor", b.VV, ct)
                    if c == 0:
                        t.matmul(ps[:, off:off + 512], b.tri[:, :],
                                 kv_tm[:, c * D + k * 512: c * D + (k + 1) * 512],
                                 start=True, stop=True).then_inc(*b.TT.step(1))
                    else:
                        t.matmul(ps[:, off:off + 512], b.tri[:, :],
                                 kv_tm[:, c * D + k * 512: c * D + (k + 1) * 512],
                                 start=True, stop=False)
                        t.matmul(ps[:, off:off + 512], b.ones_row[:, :],
                                 carry[0:1, k * 512:(k + 1) * 512],
                                 start=False, stop=True).then_inc(*b.TT.step(1))
                    if c < NTC - 1:
                        if auxf is not None:
                            self.wait(t, "tensor", auxf[0], auxf[1])
                        t.matmul(aux[0:1, 0:512], b.scol[:, 0:1],
                                 kv_tm[:, c * D + k * 512: c * D + (k + 1) * 512],
                                 start=True, stop=True).then_inc(*b.TT.step(1))
                self.sec("tensor", t_fn)
                has_tot = c < NTC - 1
                mtag = b.TT.n - (1 if has_tot else 0)
                tot_tag = b.TT.n
                qc = qc_cs[0]
                qc_cs[0] += 1
                half = qc % 2
                tb = b.tmpbf[:, half * 512:(half + 1) * 512]
                prev = self._tags.get(("l0csq", qc - 2))

                def v_fn(v, c=c, k=k, ps=ps, off=off, mtag=mtag, tot_tag=tot_tag,
                         tb=tb, prev=prev, aux=aux, has_tot=has_tot, xs=xs):
                    self.wait(v, "vector", b.XD, xs)
                    self.wait(v, "vector", b.TT, tot_tag)
                    if prev:
                        self.wait(v, "vector", b.TT, prev)
                    if has_tot:
                        if c == 0:
                            v.tensor_copy(out=carry[0:1, k * 512:(k + 1) * 512],
                                          in_=aux[0:1, 0:512]).then_inc(*b.VV.step(1))
                        else:
                            v.tensor_tensor(out=carry[0:1, k * 512:(k + 1) * 512],
                                            in0=carry[0:1, k * 512:(k + 1) * 512],
                                            in1=aux[0:1, 0:512],
                                            op=OP.add).then_inc(*b.VV.step(1))
                    else:
                        v.nop().then_inc(*b.VV.step(1))
                    v.tensor_copy(out=tb, in_=ps[:, off:off + 512]).then_inc(*b.VV.step(1))
                self.sec("vector", v_fn)
                car_tags[(c, k)] = b.VV.n - 1
                slot_free[k] = (b.VV, b.VV.n)
                aux_cs_free[(c * 4 + k) % 2] = (b.VV, b.VV.n - 1)
                vtag = b.VV.n
                self.transpose_group(
                    lambda j, tb=tb: tb[:, j * 128:(j + 1) * 128],
                    kvt3[:, k * 4:(k + 1) * 4, c * 128:(c + 1) * 128],
                    4, wait=(b.VV, vtag), tag_key=("l0csq", qc))
        self.pfree["AUX0"] = aux_cs_free[0]
        self.pfree["AUX1"] = aux_cs_free[1]
        self.barrier()
        self.dump("kvt", kvt_fm[:, :], [128, NDC * T])

        xr = b.XD.inc(16)

        def s_in(sync):
            self.wait(sync, "sync", b.CC, cct)
            sync.dma_start(out=b.craw[:, :],
                           in_=b.cc_car_out.ap().rearrange("o (c p) -> (o c) p", p=128)
                           ).then_inc(b.XD.h, 16)
        self.sec("sync", s_in)

        g = self._tg_i
        self._tg_i += 1
        trp = b.TRP[g % 2]

        def t_cc(t, g=g, trp=trp):
            self.wait(t, "tensor", b.XD, xr)
            h = self._tg_hist.get(g - 2)
            if h:
                self.wait(t, "tensor", h["evctr"], h["evtag"])
            t.transpose(trp[:, 0:16], b.craw[:, :],
                        b.identb[0:16, 0:16]).then_inc(*b.TT.step(1))
        self.sec("tensor", t_cc)
        tcc = b.TT.n

        def v_cc(v, trp=trp, tcc=tcc):
            self.wait(v, "vector", b.TT, tcc)
            v.tensor_scalar(out=b.c_cols[:, :], in0=trp[:, 0:16],
                            scalar1=b.cmask[:, 0:1], scalar2=None,
                            op0=OP.mult).then_inc(*b.VV.step(1))
        self.sec("vector", v_cc)
        self._tg_hist[g] = {"evctr": b.VV, "evtag": b.VV.n}
        ccols_tag = b.VV.n
        self.dump("ccols", b.c_cols[:, :], [128, 16])

        for i in range(NDC):
            def v_fn(v, i=i):
                self.wait(v, "vector", b.VV, ccols_tag)
                v.scalar_tensor_tensor(out=q_fm[:, i * T:(i + 1) * T],
                                       in0=kvt_fm[:, i * T:(i + 1) * T],
                                       scalar=b.c_cols[:, i:i + 1],
                                       in1=q_fm[:, i * T:(i + 1) * T],
                                       op0=OP.add, op1=OP.mult).then_inc(*b.VV.step(1))
            self.sec("vector", v_fn)
        self.barrier()
        self.dump("vhat", q_fm[:, :], [128, NDC * T])

        out_fm = kv_tm
        with ExitStack() as psc:
            slabs_o = [self.sb(f"l0a_so{i}", [128, D], BF16, psc) for i in range(2)]
            wst_o = [self.sb(f"l0a_wo{i}", [128, D], BF16, psc) for i in range(2)]
            st = {"kind": "rowT", "w": b.dram["L0_attn_Wo"], "slabs": slabs_o,
                  "wstage": wst_o, "alt": True}

            def on_f(f, psums, mm_tags):
                def s_fn(s, f=f, ps=psums[0], mt=mm_tags[0]):
                    self.wait(s, "scalar", b.TT, mt)
                    s.copy(out=out_fm[:, f * T:(f + 1) * T],
                           in_=ps[:, :]).then_inc(*b.SS.step(1))
                self.sec("scalar", s_fn)
                self.pfree["ACC" + str(f % 2)] = (b.SS, b.SS.n)

            self.ws_pass([st], NDC, NDC, lambda d: q_fm[:, d * T:(d + 1) * T], on_f)
        self.barrier()
        self.dump("outfm", out_fm[:, :], [128, NDC * T])

        out_tm = q_fm
        self.fm_to_tm(out_fm, out_tm)
        self.barrier()
        self.post_ln_residual(out_tm)
        self.barrier()

    def block_l0_ffn(self, sc):
        b = self
        xn = self.sb("l0f_xn", [128, NDC * T], BF16, sc)
        h1 = self.sb("l0f_h1", [128, 32 * T], BF16, sc)
        h2b = self.sb("l0f_h2b", [128, 16 * T], BF16, sc)

        self.pre_ln(xn)
        self.barrier()

        def h2_chunk(j):
            return (xn[:, j * T:(j + 1) * T] if j < 16
                    else h2b[:, (j - 16) * T:(j - 15) * T])

        with ExitStack() as psc:
            slabs = [self.sb(f"l0f_s1{i}", [128, D], BF16, psc) for i in range(2)]
            st = {"kind": "staged", "w": "L0_ffn_W1", "slabs": slabs, "alt": True}

            def on_f(f, psums, mm_tags):
                def s_fn(s, f=f, ps=psums[0], mt=mm_tags[0]):
                    self.wait(s, "scalar", b.TT, mt)
                    s.activation(out=h1[:, f * T:(f + 1) * T], in_=ps[:, :],
                                 func=AF.Relu, bias=b.cols["b1"][:, f:f + 1],
                                 scale=1.0).then_inc(*b.SS.step(1))
                self.sec("scalar", s_fn)
                self.pfree["ACC" + str(f % 2)] = (b.SS, b.SS.n)

            self.ws_pass([st], 32, NDC, lambda d: xn[:, d * T:(d + 1) * T], on_f)
        self.barrier()

        with ExitStack() as psc:
            slabs = [self.sb(f"l0f_s2{i}", [128, D], BF16, psc) for i in range(2)]
            st = {"kind": "staged", "w": "L0_ffn_W2", "slabs": slabs, "alt": True}

            def on_f(f, psums, mm_tags):
                def s_fn(s, f=f, ps=psums[0], mt=mm_tags[0]):
                    self.wait(s, "scalar", b.TT, mt)
                    s.activation(out=h2_chunk(f), in_=ps[:, :],
                                 func=AF.Relu, bias=b.cols["b2"][:, f:f + 1],
                                 scale=1.0).then_inc(*b.SS.step(1))
                self.sec("scalar", s_fn)
                self.pfree["ACC" + str(f % 2)] = (b.SS, b.SS.n)

            self.ws_pass([st], 32, 32, lambda d: h1[:, d * T:(d + 1) * T], on_f)
        self.barrier()

        out_fm = h1
        with ExitStack() as psc:
            slabs = [self.sb(f"l0f_s3{i}", [128, D], BF16, psc) for i in range(2)]
            st = {"kind": "staged", "w": "L0_ffn_W3", "slabs": slabs, "alt": True}

            def on_f(f, psums, mm_tags):
                def v_fn(v, f=f, ps=psums[0], mt=mm_tags[0]):
                    self.wait(v, "vector", b.TT, mt)
                    v.tensor_scalar(out=out_fm[:, f * T:(f + 1) * T], in0=ps[:, :],
                                    scalar1=b.cols["b3"][:, f:f + 1], scalar2=None,
                                    op0=OP.add).then_inc(*b.VV.step(1))
                self.sec("vector", v_fn)
                self.pfree["ACC" + str(f % 2)] = (b.VV, b.VV.n)

            self.ws_pass([st], NDC, 32, h2_chunk, on_f)
        self.barrier()

        out_tm = h1[:, 16 * T:32 * T]
        self.fm_to_tm(out_fm, out_tm)
        self.barrier()
        self.post_ln_residual(out_tm)
        self.barrier()

    def block_l1_attn(self, sc):
        b = self
        xn = self.sb("l1a_xn", [128, NDC * T], BF16, sc)
        qp_fm = self.sb("l1a_qp", [128, NDC * T], BF16, sc)
        attn_fm = self.sb("l1a_at", [128, NDC * T], BF16, sc)
        kv_cc = self.sb("l1a_kvcc", [128, 16 * 129], BF16, sc)
        tA = self.sb("l1a_tA", [128, T], F32, sc)
        tB = self.sb("l1a_tB", [128, T], F32, sc)
        kptm2 = attn_fm[:, 0:2 * T]
        vtm2 = attn_fm[:, 2 * T:4 * T]
        kbf2 = attn_fm[:, 4 * T:6 * T]
        vbf2 = attn_fm[:, 6 * T:8 * T]
        kraw = self.sb("l1a_kraw", [128, T], BF16, sc)
        tatt = self.sb("l1a_tatt", [128, 128], BF16, sc)
        b.zbuf = self.sb("l1a_zbuf", [128, 16], F32, sc)
        b.tatt2 = self.sb("l1a_tatt2", [128, 768], BF16, sc)

        self.pre_ln(xn)
        self.barrier()

        kv_cct = [None, None]
        kv_xr = [None, None]

        def kv_recv(half):
            lo, hi = half * (8 * 129), half * (8 * 129) + 8 * 129
            cct = kv_cct[half]
            xr = b.XD.inc(16)

            def s_in(sync, cct=cct, half=half, lo=lo, hi=hi):
                self.wait(sync, "sync", b.CC, cct)
                sync.dma_start(out=kv_cc[:, lo:hi],
                               in_=b.cc_kv_out[half][:, :]).then_inc(b.XD.h, 16)
            self.sec("sync", s_in)
            kv_xr[half] = xr

        with ExitStack() as psc:
            slabs_k = [self.sb(f"l1a_sk{i}", [128, D], BF16, psc) for i in range(2)]
            slabs_v = [self.sb(f"l1a_sv{i}", [128, D], BF16, psc) for i in range(2)]
            streams = [
                {"kind": "staged", "w": "L1_attn_k_w", "slabs": slabs_k,
                 "psum": b.ACC[0], "pkey": "ACC0"},
                {"kind": "staged", "w": "L1_attn_v_w", "slabs": slabs_v,
                 "psum": b.ACC[1], "pkey": "ACC1"},
            ]
            head_state = {}

            def emit_back(h):
                st_h = head_state[h]
                ho = (h % 2) * T
                ev1 = self.transpose_group(
                    lambda j, ho=ho: kbf2[:, ho + j * 128:ho + (j + 1) * 128],
                    kptm2[:, ho:ho + T], 8, wait=st_h["kready"])
                ev2 = self.transpose_group(
                    lambda j, ho=ho: vbf2[:, ho + j * 128:ho + (j + 1) * 128],
                    vtm2[:, ho:ho + T], 8, wait=st_h["vready"])
                st_h["ev"] = [ev1, ev2]
                aux = b.AUX[h % 2]

                def t_fn(t, h=h, ho=ho, aux=aux, ev1=ev1, ev2=ev2):
                    self.wait(t, "tensor", ev1[0], ev1[1])
                    self.wait(t, "tensor", ev2[0], ev2[1])
                    self.pe_wait_pfree(t, "AUX" + str(h % 2))
                    for scn in range(NTC):
                        t.matmul(aux[:, 0:128],
                                 kptm2[:, ho + scn * 128:ho + (scn + 1) * 128],
                                 vtm2[:, ho + scn * 128:ho + (scn + 1) * 128],
                                 start=(scn == 0), stop=False)
                        last = (scn == NTC - 1)
                        mm = t.matmul(aux[:, 128:129],
                                      kptm2[:, ho + scn * 128:ho + (scn + 1) * 128],
                                      b.ones_col[:, 0:1], start=False, stop=last)
                        if last:
                            mm.then_inc(*b.TT.step(1))
                self.sec("tensor", t_fn)
                kvmm = b.TT.n
                st_h["kvmm"] = kvmm

                def v3_fn(v, h=h, aux=aux, kvmm=kvmm):
                    self.wait(v, "vector", b.TT, kvmm)
                    v.tensor_copy(out=kv_cc[:, h * 129:(h + 1) * 129],
                                  in_=aux[:, 0:129]).then_inc(*b.VV.step(1))
                self.sec("vector", v3_fn)
                self.pfree["AUX" + str(h % 2)] = (b.VV, b.VV.n)
                st_h["v3"] = b.VV.n

            def on_f(h, psums, mm_tags):
                mt = max(mm_tags)
                prev_exp = head_state.get(h - 1, {}).get("exp", 0)
                prev_v2 = head_state.get(h - 1, {}).get("v2", 0)
                evp = head_state.get(h - 2, {}).get("ev")
                ho = (h % 2) * T

                def se_fn(s, h=h, ho=ho, mt=mt, prev_exp=prev_exp,
                          prev_v2=prev_v2, evp=evp):
                    self.wait(s, "scalar", b.TT, mt)
                    self.wait(s, "scalar", b.SS, prev_exp)
                    self.wait(s, "scalar", b.VV, prev_v2)
                    if evp:
                        for e in evp:
                            self.wait(s, "scalar", e[0], e[1])
                    s.activation(out=kraw[:, :], in_=b.ACC[0][:, :],
                                 func=AF.Copy).then_inc(*b.SS.step(1))
                    s.activation(out=vbf2[:, ho:ho + T], in_=b.ACC[1][:, :],
                                 func=AF.Identity,
                                 bias=b.cols["vb"][:, h:h + 1]).then_inc(*b.SS.step(1))
                self.sec("scalar", se_fn)
                self.pfree["ACC0"] = (b.SS, b.SS.n - 1)
                self.pfree["ACC1"] = (b.SS, b.SS.n)
                sevac = b.SS.n

                def v_fn(v, h=h, sevac=sevac, prev_v2=prev_v2):
                    self.wait(v, "vector", b.VV, prev_v2)
                    self.wait(v, "vector", b.SS, sevac)
                    v.tensor_scalar(out=tA[:, :], in0=kraw[:, :],
                                    scalar1=b.cols["kb"][:, h:h + 1], scalar2=0.0,
                                    op0=OP.add, op1=OP.min).then_inc(*b.VV.step(1))
                    v.tensor_scalar(out=tB[:, :], in0=kraw[:, :],
                                    scalar1=b.cols["kb"][:, h:h + 1], scalar2=0.0,
                                    op0=OP.add, op1=OP.max).then_inc(*b.VV.step(1))
                self.sec("vector", v_fn)
                vfn_done = b.VV.n

                def s_fn(s, vfn_done=vfn_done):
                    self.wait(s, "scalar", b.VV, vfn_done)
                    s.activation(out=kraw[:, :], in_=tA[:, :],
                                 func=AF.Exp).then_inc(*b.SS.step(1))
                self.sec("scalar", s_fn)
                exp_tag = b.SS.n
                stag = b.SS.n

                def v2_fn(v, h=h, ho=ho, stag=stag, vfn_done=vfn_done, evp=evp):
                    self.wait(v, "vector", b.VV, vfn_done)
                    self.wait(v, "vector", b.SS, stag)
                    if evp:
                        self.wait(v, "vector", evp[0][0], evp[0][1])
                    v.tensor_add(kbf2[:, ho:ho + T], kraw[:, :],
                                 tB[:, :]).then_inc(*b.VV.step(1))
                self.sec("vector", v2_fn)
                kready = b.VV.n
                head_state.setdefault(h, {}).update(
                    {"v2": kready, "exp": exp_tag,
                     "kready": (b.VV, kready), "vready": (b.SS, sevac)})
                if h > 0:
                    emit_back(h - 1)
                if h - 1 == 7:
                    kv_exchange(0)

            HB = 8 * 129

            def kv_exchange(half):
                lo, hi = half * HB, half * HB + HB
                v3t = head_state[half * 8 + 7]["v3"]
                xs = b.XD.inc(16)

                def s_out(sync, v3t=v3t, half=half, lo=lo, hi=hi):
                    self.wait(sync, "sync", b.VV, v3t)
                    sync.dma_start(out=b.cc_kv_in[half][:, :],
                                   in_=kv_cc[:, lo:hi]).then_inc(b.XD.h, 16)
                self.sec("sync", s_out)

                def g_cc(g, xs=xs, half=half):
                    g.wait_ge(b.XD.h, xs)
                    g.collective_compute(
                        "AllReduce", OP.add,
                        replica_groups=[[0, 1], [2, 3], [4, 5], [6, 7]],
                        ins=[b.cc_kv_in[half].ap()], outs=[b.cc_kv_out[half].ap()],
                    ).then_inc(*b.CC.step(1))
                self.sec("gpsimd", g_cc)
                kv_cct[half] = b.CC.n

            self.ws_pass(streams, NDC, NDC, lambda d: xn[:, d * T:(d + 1) * T], on_f)
            emit_back(NDC - 1)
            kv_exchange(1)
        self.barrier()
        self.dump("l1kvcc", kv_cc[:, :], [128, 16 * 129])

        with ExitStack() as psc:
            slabs_q = [self.sb(f"l1a_sq{i}", [128, D], BF16, psc) for i in range(2)]
            st = {"kind": "staged", "w": "L1_attn_q_w", "slabs": slabs_q, "alt": True}
            qp_state = {}

            def on_f(f, psums, mm_tags):
                prev_exp = qp_state.get(f - 1, {}).get("exp", 0)
                prev_v2 = qp_state.get(f - 1, {}).get("v2", 0)

                def v_fn(v, f=f, ps=psums[0], mt=mm_tags[0], prev_exp=prev_exp,
                         prev_v2=prev_v2):
                    self.wait(v, "vector", b.VV, prev_v2)
                    self.wait(v, "vector", b.TT, mt)
                    self.wait(v, "vector", b.SS, prev_exp)
                    v.tensor_scalar(out=tA[:, :], in0=ps[:, :],
                                    scalar1=b.cols["qb"][:, f:f + 1], scalar2=0.0,
                                    op0=OP.add, op1=OP.min).then_inc(*b.VV.step(1))
                    v.tensor_scalar(out=tB[:, :], in0=ps[:, :],
                                    scalar1=b.cols["qb"][:, f:f + 1], scalar2=0.0,
                                    op0=OP.add, op1=OP.max).then_inc(*b.VV.step(1))
                self.sec("vector", v_fn)
                self.pfree["ACC" + str(f % 2)] = (b.VV, b.VV.n)
                vfn_done = b.VV.n
                va = b.VV.n - 1

                def s_fn(s, vfn_done=vfn_done):
                    self.wait(s, "scalar", b.VV, vfn_done)
                    s.activation(out=kraw[:, :], in_=tA[:, :],
                                 func=AF.Exp).then_inc(*b.SS.step(1))
                self.sec("scalar", s_fn)
                qp_state[f] = {"exp": b.SS.n}
                stag = b.SS.n

                def v2_fn(v, f=f, stag=stag, vfn_done=vfn_done):
                    self.wait(v, "vector", b.VV, vfn_done)
                    self.wait(v, "vector", b.SS, stag)
                    v.tensor_add(qp_fm[:, f * T:(f + 1) * T], kraw[:, :],
                                 tB[:, :]).then_inc(*b.VV.step(1))
                self.sec("vector", v2_fn)
                qp_state[f]["v2"] = b.VV.n

            self.ws_pass([st], NDC, NDC, lambda d: xn[:, d * T:(d + 1) * T], on_f)
            kv_recv(0)
            kv_recv(1)
        self.barrier()
        self.dump("l1qp", qp_fm[:, :], [128, NDC * T])
        self.dump("l1kvsum", kv_cc[:, :], [128, 16 * 129])

        banks4 = [(b.AUX[0], 0, "S0"), (b.AUX[1], 0, "S1"),
                  (b.ACC[0], 0, "S2"), (b.ACC[0], 512, "S3")]
        gi = 0
        for h in range(NDC):
            for g0 in range(3):
                n3 = (3, 3, 2)[g0]
                sc0 = (0, 3, 6)[g0]
                ps4, off4, pk = banks4[gi % 4]
                ta = b.tatt2[:, (gi % 2) * 384:(gi % 2) * 384 + 384]
                zb = b.zbuf[:, (gi % 2) * 4:(gi % 2) * 4 + 3]
                zr = b.zbuf[:, 8 + (gi % 2) * 4: 8 + (gi % 2) * 4 + 3]

                def t_fn(t, h=h, sc0=sc0, n3=n3, ps4=ps4, off4=off4, pk=pk):
                    self.wait(t, "tensor", b.XD, kv_xr[0] if h < 8 else kv_xr[1])
                    self.pe_wait_pfree(t, pk)
                    for j in range(n3):
                        scn = sc0 + j
                        t.matmul(ps4[:, off4 + j * 129: off4 + (j + 1) * 129],
                                 qp_fm[:, h * T + scn * 128: h * T + (scn + 1) * 128],
                                 kv_cc[:, h * 129:(h + 1) * 129],
                                 start=True, stop=True).then_inc(*b.TT.step(1))
                self.sec("tensor", t_fn)
                mtag = b.TT.n
                prev_t = self._tags.get(("attg", gi - 2))

                def v_fn(v, n3=n3, ps4=ps4, off4=off4, mtag=mtag, prev_t=prev_t,
                         ta=ta, zb=zb, zr=zr):
                    self.wait(v, "vector", b.TT, mtag)
                    if prev_t:
                        self.wait(v, "vector", b.TT, prev_t)
                    v.tensor_scalar(out=zb[:, 0:n3],
                                    in0=ps4[:, off4 + 128: off4 + 128 + (n3 - 1) * 129 + 1:129],
                                    scalar1=Z_EPS, scalar2=None,
                                    op0=OP.add).then_inc(*b.VV.step(1))
                    self.wait(v, "vector", b.VV, b.VV.n)
                    v.reciprocal(out=zr[:, 0:n3], in_=zb[:, 0:n3]).then_inc(*b.VV.step(1))
                    self.wait(v, "vector", b.VV, b.VV.n)
                    for j in range(n3):
                        v.tensor_scalar(out=ta[:, j * 128:(j + 1) * 128],
                                        in0=ps4[:, off4 + j * 129: off4 + j * 129 + 128],
                                        scalar1=zr[:, j:j + 1], scalar2=None,
                                        op0=OP.mult).then_inc(*b.VV.step(1))
                self.sec("vector", v_fn)
                self.pfree[pk] = (b.VV, b.VV.n)
                vtag = b.VV.n
                self.transpose_group(
                    lambda j, ta=ta: ta[:, j * 128:(j + 1) * 128],
                    attn_fm[:, h * T + sc0 * 128: h * T + (sc0 + n3) * 128],
                    n3, wait=(b.VV, vtag), tag_key=("attg", gi))
                gi += 1
        self.barrier()
        self.dump("l1attn", attn_fm[:, :], [128, NDC * T])

        out_fm = xn
        with ExitStack() as psc:
            slabs_o = [self.sb(f"l1a_so{i}", [128, D], BF16, psc) for i in range(2)]
            st = {"kind": "staged", "w": "L1_attn_o_w", "slabs": slabs_o, "alt": True}

            def on_f(f, psums, mm_tags):
                def v_fn(v, f=f, ps=psums[0], mt=mm_tags[0]):
                    self.wait(v, "vector", b.TT, mt)
                    v.tensor_scalar(out=out_fm[:, f * T:(f + 1) * T], in0=ps[:, :],
                                    scalar1=b.cols["ob"][:, f:f + 1], scalar2=None,
                                    op0=OP.add).then_inc(*b.VV.step(1))
                self.sec("vector", v_fn)
                self.pfree["ACC" + str(f % 2)] = (b.VV, b.VV.n)

            self.ws_pass([st], NDC, NDC, lambda d: attn_fm[:, d * T:(d + 1) * T], on_f)
        self.barrier()

        out_tm = qp_fm
        self.fm_to_tm(out_fm, out_tm)
        self.barrier()
        self.post_ln_residual(out_tm)
        self.barrier()

    def block_l1_ffn(self, sc):
        b = self
        I6 = 6144
        NIC = I6 // 128
        xn = self.sb("l1f_xn", [128, NDC * T], BF16, sc)
        fuse = self.sb("l1f_fuse", [128, NIC * 512], BF16, sc)
        out_fm = self.sb("l1f_out", [128, NDC * T], BF16, sc)

        self.pre_ln(xn)
        self.barrier()

        for half in range(2):
            toff = half * 512

            def fuse_chunk(j):
                return fuse[:, j * 512:(j + 1) * 512]

            with ExitStack() as psc:
                slabs_g = [self.sb(f"l1f_sg{half}{i}", [128, D], BF16, psc)
                           for i in range(2)]
                slabs_u = [self.sb(f"l1f_su{half}{i}", [128, D], BF16, psc)
                           for i in range(2)]
                streams = [
                    {"kind": "st8", "w": "L1_ffn_gate_w", "slabs": slabs_g,
                     "psum": b.ACC[0], "pkey": "ACC0"},
                    {"kind": "st8", "w": "L1_ffn_up_w", "slabs": slabs_u,
                     "psum": b.ACC[1], "pkey": "ACC1"},
                ]
                gstate = {}

                def on_f(f, psums, mm_tags):
                    mt = max(mm_tags)
                    prev_fuse = gstate.get(f - 1, {}).get("fuse", 0)

                    def s_fn(s, f=f, mt=mt, prev_fuse=prev_fuse):
                        self.wait(s, "scalar", b.TT, mt)
                        self.wait(s, "scalar", b.VV, prev_fuse)
                        s.activation(out=fuse_chunk(f), in_=b.ACC[0][:, 0:512],
                                     func=AF.Silu, scale=b.cols["gs"][:, f:f + 1],
                                     bias=0.0).then_inc(*b.SS.step(1))
                    self.sec("scalar", s_fn)
                    self.pfree["ACC0"] = (b.SS, b.SS.n)
                    stag = b.SS.n

                    def v_fn(v, f=f, stag=stag):
                        self.wait(v, "vector", b.SS, stag)
                        v.scalar_tensor_tensor(out=fuse_chunk(f),
                                               in0=b.ACC[1][:, 0:512],
                                               scalar=b.cols["us"][:, f:f + 1],
                                               in1=fuse_chunk(f), op0=OP.mult,
                                               op1=OP.mult).then_inc(*b.VV.step(1))
                    self.sec("vector", v_fn)
                    gstate[f] = {"fuse": b.VV.n}
                    self.pfree["ACC1"] = (b.VV, b.VV.n)

                self.ws_pass(streams, NIC, NDC,
                             lambda d: xn[:, d * T + toff: d * T + toff + 512],
                             on_f, t_width=512)
            self.barrier()

            with ExitStack() as psc:
                slabs_d = [self.sb(f"l1f_sd{half}{i}", [128, D], BF16, psc)
                           for i in range(2)]
                st = {"kind": "st8", "w": "L1_ffn_down_w", "slabs": slabs_d,
                      "alt": True}

                def on_f(f, psums, mm_tags):
                    def v_fn(v, f=f, ps=psums[0], mt=mm_tags[0]):
                        self.wait(v, "vector", b.TT, mt)
                        v.tensor_scalar(out=out_fm[:, f * T + toff: f * T + toff + 512],
                                        in0=ps[:, 0:512],
                                        scalar1=b.cols["ds"][:, f:f + 1], scalar2=None,
                                        op0=OP.mult).then_inc(*b.VV.step(1))
                    self.sec("vector", v_fn)
                    self.pfree["ACC" + str(f % 2)] = (b.VV, b.VV.n)

                self.ws_pass([st], NDC, NIC, fuse_chunk, on_f, t_width=512)
            self.barrier()

        out_tm = xn
        self.fm_to_tm(out_fm, out_tm)
        self.barrier()
        self.post_ln_residual(out_tm, final=(self.upto == 4))
        self.barrier()

    def finish(self):
        b = self
        self.drain_bg()
        if getattr(b, "stored_final", False):
            fin = b.XD.n

            def s_fn(sync):
                sync.wait_ge(b.XD.h, fin)
            self.sec("sync", s_fn)
        else:
            vtag = b.VV.n
            ftag = b.SS.n
            final = b.XD.inc(16)

            def s_fn(sync):
                self.wait(sync, "sync", b.VV, vtag)
                self.wait(sync, "sync", b.SS, ftag)
                sync.dma_start(
                    out=b.y_out.ap().rearrange("(c p) d -> p c d", p=128),
                    in_=b.x_tm[:, :].rearrange("p (c d) -> p c d", c=NTC),
                ).then_inc(b.XD.h, 16)
                sync.wait_ge(b.XD.h, final)
            self.sec("sync", s_fn)
        self.flush()


def build_program(upto=4, debug_dumps=False):
    nc = bass.Bass("TRN2", target_bir_lowering=False, debug=False)
    with ExitStack() as stack:
        bld = Builder(nc, stack, upto)
        bld.debug_dumps = debug_dumps
        with nc.Block() as block:
            bld.block = block
            with ExitStack() as isc:
                bld.init_phase()
                if upto >= 2:
                    bld.stage_weights(isc)
                    bld.emit_ag("w")
                bld.barrier()
            if upto >= 1:
                with ExitStack() as sc:
                    bld.block_l0_attn(sc)
            if upto >= 2:
                with ExitStack() as sc:
                    bld.block_l0_ffn(sc)
            if upto >= 3:
                with ExitStack() as sc:
                    bld.block_l1_attn(sc)
            if upto >= 4:
                with ExitStack() as sc:
                    bld.block_l1_ffn(sc)
            bld.finish()
    return nc


def make_in_maps(inputs, upto=4):
    x = np.asarray(inputs["L0_attn_x"])
    keep = ["L0_attn_Wq", "L0_attn_Wk", "L0_attn_Wv", "L0_attn_Wo",
            "L0_ffn_b1", "L0_ffn_b2", "L0_ffn_b3",
            "L1_attn_q_b", "L1_attn_k_b", "L1_attn_v_b", "L1_attn_o_b",
            "L1_ffn_gate_scale", "L1_ffn_up_scale", "L1_ffn_down_scale"]
    in_maps = []
    for c in range(NCORES):
        bidx, half = c // 2, c % 2
        m = {"x": np.ascontiguousarray(x[bidx, half * T:(half + 1) * T, :]),
             "cmask": np.full((128, 1), float(half), np.float32),
             "sendmask": np.full((1, 1), float(1 - half), np.float32)}
        for k in keep:
            m[k] = np.ascontiguousarray(np.asarray(inputs[k], np.float32))
        parts = []
        for name, F, K in STAGED_W:
            W = np.asarray(inputs[name], np.float32)
            cpc = (F // 128) // NCORES
            for lf in range(cpc):
                f = c * cpc + lf
                parts.append(np.ascontiguousarray(W[f * 128:(f + 1) * 128, :]).reshape(-1))
        m["stw"] = (np.concatenate(parts).reshape(1, SHARD) if parts
                    else np.zeros((1, SHARD), np.float32))
        parts8 = []
        for name, F, K in STAGED_8:
            W = np.asarray(inputs[name], np.float32)
            cpc = (F // 128) // NCORES
            for lf in range(cpc):
                f = c * cpc + lf
                parts8.append(np.ascontiguousarray(W[:, f * 128:(f + 1) * 128]).reshape(-1))
        m["st8"] = (np.concatenate(parts8).reshape(1, SHARD8) if parts8
                    else np.zeros((1, SHARD8), np.float32))
        in_maps.append(m)
    return in_maps


def kernel(**inputs):
    nc = build_program()
    in_maps = make_in_maps(inputs)
    res = run_bass_kernel_spmd(nc, in_maps, core_ids=list(range(NCORES)))
    y = np.empty((4, 2 * T, D), np.float32)
    for c in range(NCORES):
        bidx, half = c // 2, c % 2
        y[bidx, half * T:(half + 1) * T, :] = res.results[c]["y"]
    return y
